# version 01
# baseline (speedup 1.0000x reference)
import os
import numpy as np
import ml_dtypes
import concourse.bass as bass
import concourse.mybir as mybir
from concourse.bass_utils import run_bass_kernel_spmd

F32 = mybir.dt.float32
BF16 = mybir.dt.bfloat16
I32 = mybir.dt.int32
AF = mybir.ActivationFunctionType
ALU = mybir.AluOpType
AX = mybir.AxisListType.X

D = 4096
DC = 32
T = 1024
TH = 512
S = 2048
IC = 86
NH = 32
NKV = 8
INV_SQRT_D = float(1.0 / np.sqrt(128.0))
EPS = 1e-5
NEG = -1e9

N_SUB = int(os.environ.get("MK_NSUB", "4"))
MK_DUMP = os.environ.get("MK_DUMP", "")
MK_LOCKSTEP = os.environ.get("MK_LOCKSTEP", "1") == "1"
CHUNK_OPS = 400


class MK:
    def __init__(self, n_sub=4):
        self.n_sub = n_sub
        nc = bass.Bass("TRN2", target_bir_lowering=False)
        self.nc = nc
        self.ops = {k: [] for k in ("tensor", "vector", "scalar", "gpsimd", "sync")}
        self._ctx = []

        din = lambda n, s, d: nc.dram_tensor(n, s, d, kind="ExternalInput").ap()
        self.xin = din("xin", [D, T], F32)
        self.wqa = din("wqa", [32, 128, 32, 128], F32)
        self.wka = din("wka", [8, 128, 32, 128], F32)
        self.wva = din("wva", [8, 128, 32, 128], F32)
        self.woa = din("woa", [32, 128, 32, 128], F32)
        self.wqb = din("wqb", [32, 128, 32, 128], F32)
        self.wkrb = din("wkrb", [4, 128, 32, 128], F32)
        self.wdkb = din("wdkb", [4, 128, 32, 128], F32)
        self.wupkb = din("wupkb", [4, 128, 4, 128], F32)
        self.wupvb = din("wupvb", [8, 128, 4, 128], F32)
        self.wob = din("wob", [32, 128, 32, 128], F32)
        self.gw = din("gw", [86, 128, 32, 128], I32)
        self.uw = din("uw", [86, 128, 32, 128], I32)
        self.dwt = din("dwt", [8, 86, 128, 512], I32)
        self.masks = din("masks", [8, 128, 1536], BF16)
        self.ropa = din("ropa", [2, 128, T], BF16)
        self.ropk = din("ropk", [2, 128, T], BF16)
        self.permm = din("permm", [128, 512], BF16)
        self.identi = din("identi", [128, 128], BF16)
        self.onesb = din("onesb", [128, 2], BF16)
        self.onesf = din("onesf", [1, 128], F32)
        self.consts = din("consts", [1, 8], F32)
        self.out = nc.dram_tensor("out", [D, T], F32, kind="ExternalOutput").ap()

        self.qsp = nc.dram_tensor("qsp", [32, 128, T], BF16).ap()
        self.asp = nc.dram_tensor("asp", [D, T], BF16).ap()
        self.ccin_t = nc.dram_tensor("ccin", [2048, T], BF16)
        self.ccout_t = nc.dram_tensor("ccout", [4096, T], BF16)
        self.gcc = nc.dram_tensor("gcc", [86, 128, 32, 128], BF16).ap()
        self.ucc = nc.dram_tensor("ucc", [86, 128, 32, 128], BF16).ap()
        self.dcc = nc.dram_tensor("dcc", [8, 86, 128, 512], BF16).ap()

        self.alloc_map = {}

        def sb(name, shape, dt):
            b0 = nc.sbuf_base
            cm = nc.sbuf_tensor(name, shape, dt)
            h = cm.__enter__()
            self._ctx.append(cm)
            self.alloc_map[name] = (b0, nc.sbuf_base)
            return h

        def psmm(name, shape, dt):
            cm = nc.psum_tensor(name, shape, dt)
            h = cm.__enter__()
            self._ctx.append(cm)
            return h

        self.stash = sb("stash", [128, DC * T], BF16)
        self.muB = sb("muB", [128, T], F32)
        self.sgB = sb("sgB", [128, T], F32)
        self.scrF = sb("scrF", [128, T], F32)
        self.xsq = sb("xsq", [128, 2 * T], BF16)
        self.row = sb("row", [1, T], F32)
        self.t_id = sb("t_id", [128, 128], BF16)
        self.t_ones_b = sb("t_ones_b", [128, 2], BF16)
        self.t_ones_f = sb("t_ones_f", [1, 128], F32)
        self.t_consts = sb("t_consts", [1, 8], F32)
        self.scalB = sb("scalB", [128, 8], F32)
        self.nmx = sb("nmx", [128, 4], F32)
        base_before_us = None
        self.wsA = sb("wsA", [128, 4096], BF16)
        self.wsB = sb("wsB", [128, 4096], BF16)
        base_before_us = nc.sbuf_base
        self.usA = sb("usA", [128, 4096], BF16)
        self.usB = sb("usB", [128, 4096], BF16)
        base_after_us = nc.sbuf_base

        nc.sbuf_base = base_before_us
        self.kst = sb("kst", [128, NKV * S], BF16)
        self.vtt = sb("vtt", [128, 16 * 1024], BF16)
        self.rhoA = sb("rhoA", [128, T], F32)
        sub_base = nc.sbuf_base
        self.t_cos = sb("t_cos", [128, T], BF16)
        self.t_sin = sb("t_sin", [128, T], BF16)
        self.t_cos2 = sb("t_cos2", [128, T], BF16)
        self.t_sin2 = sb("t_sin2", [128, T], BF16)
        self.t_perm = sb("t_perm", [128, 512], BF16)
        self.kr0 = sb("kr0", [128, 4 * T], BF16)
        self.ckv = sb("ckv", [128, 4 * T], BF16)
        self.vtmp = sb("vtmp", [128, T], BF16)
        self.qtm = [sb("qtmA", [128, T], BF16), sb("qtmB", [128, T], BF16)]
        self.tr1 = sb("tr1", [128, T], BF16)
        self.tr2 = sb("tr2", [128, T], BF16)
        self.kcs_tiles = [self.tr1, self.tr2, self.qtm[0], self.qtm[1]]
        qkv_end = nc.sbuf_base
        nc.sbuf_base = sub_base
        self.ptt = sb("ptt", [128, 16 * 512], BF16)
        self.psb = sb("psb", [128, 2048], BF16)
        self.mstg = sb("mstg", [128, 4 * 1536], BF16)
        self.qbuf = sb("qbuf", [128, 2 * 512], BF16)
        self.abuf = sb("abuf", [128, 1024], BF16)
        core_end = nc.sbuf_base
        attn_end = max(qkv_end, core_end)

        nc.sbuf_base = base_before_us
        self.aarena = sb("aarena", [128, DC * T], BF16)
        self.rhoW = sb("rhoW", [128, T], F32)
        wo_end = nc.sbuf_base

        nc.sbuf_base = base_after_us
        self.rhoF = sb("rhoF", [128, T], F32)
        nc.sbuf_base = base_after_us
        self.hst = sb("hst", [128, IC * TH], BF16)
        ffn_end = nc.sbuf_base
        self.g_sb = self.xsq

        nc.sbuf_base = max(attn_end, wo_end, ffn_end)
        assert nc.sbuf_base <= nc.sbuf_top, f"SBUF over budget: {nc.sbuf_base} > {nc.sbuf_top}"
        self.sbuf_used = nc.sbuf_base

        psum_base0 = nc.psum_base
        self.psA = psmm("psA", [128, 2048], F32)
        self.psB = psmm("psB", [128, 1024], F32)
        self.psT = [psmm("psT0", [128, 1024], BF16), psmm("psT1", [128, 1024], BF16)]
        nc.psum_base = psum_base0
        self.pGH = [psmm("pG", [128, 2048], F32), psmm("pH", [128, 2048], F32)]

        self.H = {}
        self.cnt = {}
        for nm in ("dS", "dG", "dA", "cP", "cV", "cA", "cc1", "cc2"):
            cm = nc.semaphore(nm)
            self.H[nm] = cm.__enter__()
            self._ctx.append(cm)
            self.cnt[nm] = 0

        self.ws_par = 0
        self.ws_tok = {0: None, 1: None}
        self.us_tok = {0: None, 1: None}
        self.psT_tok = [None, None]
        self.psB_tok = [None, None]
        self.vt_last_cP = None
        self.ffn_prev_down = None

    def op(self, eng, waits, inc, body):
        if MK_LOCKSTEP:
            waits = list(waits) + [self.tok(nm) for nm in ("cP", "cV", "cA")]
        waits = tuple(w for w in waits if w is not None)
        def f(e, waits=waits, inc=inc, body=body):
            for (sh, v) in waits:
                e.wait_ge(sh, v)
            ins = body(e)
            if inc is not None and ins is not None:
                ins.then_inc(inc[0], inc[1])
        self.ops[eng].append(f)

    def opi(self, eng, waits, nm, body, k=1):
        self.op(eng, waits, (self.H[nm], k), body)
        self.cnt[nm] += k
        return self.cnt[nm]

    def tok(self, nm):
        return (self.H[nm], self.cnt[nm])

    def st_ch(self, c):
        return self.stash[:, c * T:(c + 1) * T]

    def st_chh(self, c, th):
        return self.stash[:, c * T + th * TH: c * T + (th + 1) * TH]

    def setup(self):
        for dst, src in ((self.t_id[:], self.identi), (self.t_ones_b[:], self.onesb),
                         (self.t_ones_f[:], self.onesf), (self.t_consts[:], self.consts)):
            self.opi("sync", [], "dS", lambda e, d=dst, s=src: e.dma_start(out=d, in_=s), 16)
        v = self.tok("dS")
        self.opi("tensor", [v], "cP",
                 lambda e: e.matmul(self.psB[:, 0:8], self.t_ones_f[:1, :], self.t_consts[:1, :],
                                    start=True, stop=True))
        self.opi("vector", [self.tok("cP")], "cV",
                 lambda e: e.tensor_copy(self.scalB[:], self.psB[:, 0:8]))
        self.scal_cv = self.tok("cV")

    def emit_ln(self, rho, first=False):
        H = self.H
        if first:
            for c in range(DC):
                self.opi("gpsimd", [], "dG",
                         lambda e, c=c: e.dma_start(out=self.st_ch(c), in_=self.xin[c * 128:(c + 1) * 128, :]), 16)
            in_tok = self.tok("dG")
        else:
            in_tok = self.tok("cV")
        stats_cp = {}
        for c in range(DC):
            sl = (c & 1) * T
            w = [in_tok] if c == 0 else []
            if c >= 2:
                w.append((H["cP"], stats_cp[c - 2]))
            ca = self.opi("scalar", w, "cA",
                          lambda e, c=c, sl=sl: e.activation(self.xsq[:, sl:sl + T], self.st_ch(c), AF.Square))
            def stats(e, c=c, sl=sl):
                last = None
                for t2 in range(2):
                    last = e.matmul(self.psA[0:1, t2 * 512:(t2 + 1) * 512], self.t_ones_b[:, 0:1],
                                    self.st_chh(c, t2), start=(c == 0), stop=(c == DC - 1))
                for t2 in range(2):
                    last = e.matmul(self.psB[0:1, t2 * 512:(t2 + 1) * 512], self.t_ones_b[:, 0:1],
                                    self.xsq[:, sl + t2 * 512: sl + (t2 + 1) * 512],
                                    start=(c == 0), stop=(c == DC - 1))
                return last
            stats_cp[c] = self.opi("tensor", [(H["cA"], ca)], "cP", stats)
        self.opi("vector", [self.tok("cP")], "cV",
                 lambda e: e.tensor_copy(self.row[:], self.psA[0:1, 0:1024]))
        def bsum(e):
            e.matmul(self.psA[:, 1024:1536], self.t_ones_f[:1, :], self.row[:1, 0:512], start=True, stop=True)
            return e.matmul(self.psA[:, 1536:2048], self.t_ones_f[:1, :], self.row[:1, 512:1024],
                            start=True, stop=True)
        self.opi("tensor", [self.tok("cV")], "cP", bsum)
        self.opi("vector", [self.tok("cP")], "cV",
                 lambda e: e.tensor_scalar_mul(self.muB[:], self.psA[:, 1024:2048], 1.0 / D))
        self.opi("vector", [], "cV", lambda e: e.tensor_copy(self.row[:], self.psB[0:1, 0:1024]))
        def bsq(e):
            e.matmul(self.psB[:, 0:512], self.t_ones_f[:1, :], self.row[:1, 0:512], start=True, stop=True)
            return e.matmul(self.psB[:, 512:1024], self.t_ones_f[:1, :], self.row[:1, 512:1024],
                            start=True, stop=True)
        self.opi("tensor", [self.tok("cV")], "cP", bsq)
        self.opi("vector", [], "cV", lambda e: e.tensor_mul(self.scrF[:], self.muB[:], self.muB[:]))
        self.opi("vector", [self.tok("cP")], "cV",
                 lambda e, rho=rho: e.scalar_tensor_tensor(rho[:], self.psB[:], 1.0 / D, self.scrF[:],
                                                           ALU.mult, ALU.subtract))
        self.opi("scalar", [self.tok("cV"), self.scal_cv], "cA",
                 lambda e, rho=rho: e.activation(self.sgB[:], rho[:], AF.Sqrt, bias=self.scalB[:, 4:5]))
        self.opi("vector", [self.tok("cA")], "cV", lambda e, rho=rho: e.reciprocal(rho[:], self.sgB[:]))
        rv = self.tok("cV")
        for c in range(DC):
            self.op("vector", [rv] if c == 0 else [], None,
                    lambda e, c=c: e.tensor_sub(self.scrF[:], self.st_ch(c), self.muB[:]))
            self.opi("vector", [], "cV",
                     lambda e, c=c, rho=rho: e.tensor_mul(self.st_ch(c), self.scrF[:], rho[:]))
        return self.tok("cV")

    def emit_proj(self, wdram, n_chunks, rhs_fn, evac_fn, y_tok, cast=True):
        H = self.H
        kc = wdram.shape[2]
        for m in range(n_chunks):
            par = self.ws_par & 1
            self.ws_par += 1
            stg = self.wsA if par == 0 else self.wsB
            dsem = "dG" if cast else "dS"
            dv = self.opi("gpsimd" if cast else "sync", [self.ws_tok[par]], dsem,
                          lambda e, m=m, stg=stg, kc=kc, wdram=wdram:
                          e.dma_start(out=stg[:, 0:kc * 128], in_=wdram[m]), 16)
            for t2 in range(2):
                pa = self.psB[:, t2 * 512:(t2 + 1) * 512]
                w = [(H[dsem], dv), y_tok, self.psB_tok[t2]]
                def mms(e, m=m, t2=t2, stg=stg, pa=pa, kc=kc, rhs_fn=rhs_fn):
                    last = None
                    for c in range(kc):
                        last = e.matmul(pa, stg[:, c * 128:(c + 1) * 128], rhs_fn(c, t2),
                                        start=(c == 0), stop=(c == kc - 1))
                    return last
                pvv = self.opi("tensor", w, "cP", mms)
                if t2 == 1:
                    self.ws_tok[par] = (H["cP"], pvv)
                self.psB_tok[t2] = evac_fn(m, t2, pa, [(H["cP"], pvv)])

    def emit_rope(self, dst, cosT, sinT, waits):
        self.op("vector", waits, None,
                lambda e, dst=dst: e.tensor_mul(self.tr1[0:64, :], dst[64:128, :], sinT[64:128, :]))
        self.op("vector", [], None,
                lambda e, dst=dst: e.tensor_mul(self.tr1[64:128, :], dst[0:64, :], sinT[0:64, :]))
        self.op("vector", [], None, lambda e, dst=dst: e.tensor_mul(self.tr2[:], dst[:], cosT[:]))
        return self.opi("vector", [], "cV",
                        lambda e, dst=dst: e.tensor_add(dst[:], self.tr2[:], self.tr1[:]))

    def emit_vt_transpose(self, m, src, ca_tok):
        H = self.H
        for b4 in range(2):
            pt = self.psT[b4]
            def trs(e, b4=b4, pt=pt, src=src):
                last = None
                for k in range(4):
                    tc = b4 * 4 + k
                    last = e.transpose(pt[:, k * 128:(k + 1) * 128], src[:, tc * 128:(tc + 1) * 128],
                                       self.t_id[:])
                return last
            pvv = self.opi("tensor", [ca_tok, self.psT_tok[b4]], "cP", trs)
            self.vt_last_cP = (H["cP"], pvv)
            def ev(e, m=m, b4=b4, pt=pt):
                dst = bass.AP(self.vtt, (b4 * 4) * 1024 + m * 128,
                              [[16 * 1024, 128], [1024, 4], [1, 128]])
                return e.tensor_copy(dst, pt[:, 0:512].rearrange("p (c n) -> p c n", c=4))
            cvv = self.opi("vector", [(H["cP"], pvv)], "cV", ev)
            self.psT_tok[b4] = (H["cV"], cvv)

    def emit_exchange(self, ccsem):
        H = self.H
        ccin = self.ccin_t.ap()
        ccout = self.ccout_t.ap()
        w0 = [self.tok("cV"), self.tok("cA")]
        for hc in range(NKV):
            self.opi("gpsimd", w0 if hc == 0 else [], "dG",
                     lambda e, hc=hc, ccin=ccin: e.dma_start(
                         out=ccin[hc * 128:(hc + 1) * 128, :], in_=self.kst[:, hc * S:hc * S + T]), 16)
        for tc in range(8):
            self.opi("gpsimd", [], "dG",
                     lambda e, tc=tc, ccin=ccin: e.dma_start(
                         out=ccin[1024 + tc * 128:1024 + (tc + 1) * 128, :],
                         in_=self.vtt[:, tc * 1024:(tc + 1) * 1024]), 16)
        dv = self.tok("dG")
        def coll(e):
            return e.collective_compute(
                "AllGather", ALU.bypass,
                replica_groups=[[0, 1], [2, 3], [4, 5], [6, 7]],
                ins=[self.ccin_t.ap()], outs=[self.ccout_t.ap()])
        self.op("gpsimd", [dv], (H[ccsem], 1), coll)
        self.cnt[ccsem] += 1
        for r in range(2):
            for hc in range(NKV):
                self.opi("gpsimd", [(H[ccsem], 1)] if (r == 0 and hc == 0) else [], "dG",
                         lambda e, r=r, hc=hc, ccout=ccout: e.dma_start(
                             out=self.kst[:, hc * S + r * T: hc * S + (r + 1) * T],
                             in_=ccout[2048 * r + hc * 128: 2048 * r + (hc + 1) * 128, :]), 16)
            for tc in range(8):
                self.opi("gpsimd", [], "dG",
                         lambda e, r=r, tc=tc, ccout=ccout: e.dma_start(
                             out=self.vtt[:, (8 * r + tc) * 1024:(8 * r + tc + 1) * 1024],
                             in_=ccout[2048 * r + 1024 + tc * 128: 2048 * r + 1024 + (tc + 1) * 128, :]), 16)
        return self.tok("dG")

    def emit_attn(self, layer, y_tok):
        H = self.H
        isa = layer == "a"
        tabs = [(self.t_cos[:], self.ropa[0]), (self.t_sin[:], self.ropa[1])]
        if not isa:
            tabs += [(self.t_cos2[:], self.ropk[0]), (self.t_sin2[:], self.ropk[1]),
                     (self.t_perm[:], self.permm)]
        for dst, src in tabs:
            self.opi("sync", [], "dS", lambda e, d=dst, s=src: e.dma_start(out=d, in_=s), 16)
        tab_tok = self.tok("dS")

        if isa:
            def k_evac(m, t2, pa, w):
                dst = self.kst[:, m * S + t2 * 512: m * S + (t2 + 1) * 512]
                ca = self.opi("scalar", w, "cA", lambda e, dst=dst, pa=pa: e.activation(dst, pa, AF.Copy))
                return (H["cA"], ca)
            self.emit_proj(self.wka, 8, lambda c, t2: self.st_chh(c, t2), k_evac, y_tok)
            first = [tab_tok, self.tok("cA")]
            for m in range(NKV):
                self.emit_rope(self.kst[:, m * S:m * S + T], self.t_cos, self.t_sin, first)
                first = []
            def v_evac(m, t2, pa, w):
                ww = list(w) + ([self.vt_last_cP] if t2 == 0 else [])
                ca = self.opi("scalar", ww, "cA",
                              lambda e, t2=t2, pa=pa: e.activation(self.vtmp[:, t2 * 512:(t2 + 1) * 512],
                                                                   pa, AF.Copy))
                if t2 == 1:
                    self.emit_vt_transpose(m, self.vtmp, (H["cA"], ca))
                return (H["cA"], ca)
            self.emit_proj(self.wva, 8, lambda c, t2: self.st_chh(c, t2), v_evac, y_tok)
        else:
            def ckv_evac(m, t2, pa, w):
                dst = self.ckv[:, m * T + t2 * 512: m * T + (t2 + 1) * 512]
                ca = self.opi("scalar", w, "cA", lambda e, dst=dst, pa=pa: e.activation(dst, pa, AF.Copy))
                return (H["cA"], ca)
            self.emit_proj(self.wdkb, 4, lambda c, t2: self.st_chh(c, t2), ckv_evac, y_tok)
            ckv_tok = self.tok("cA")
            ckv_rhs = lambda c, t2: self.ckv[:, c * T + t2 * 512: c * T + (t2 + 1) * 512]
            def kr_evac(m, t2, pa, w):
                dst = self.kr0[:, m * T + t2 * 512: m * T + (t2 + 1) * 512]
                ca = self.opi("scalar", w, "cA", lambda e, dst=dst, pa=pa: e.activation(dst, pa, AF.Copy))
                return (H["cA"], ca)
            self.emit_proj(self.wkrb, 4, lambda c, t2: self.st_chh(c, t2), kr_evac, y_tok)
            first = [tab_tok, self.tok("cA")]
            for m in range(4):
                self.emit_rope(self.kr0[:, m * T:(m + 1) * T], self.t_cos2, self.t_sin2, first)
                first = []
            kr_cv = self.tok("cV")
            def v_evac(m, t2, pa, w):
                ww = list(w) + ([self.vt_last_cP] if t2 == 0 else [])
                ca = self.opi("scalar", ww, "cA",
                              lambda e, t2=t2, pa=pa: e.activation(self.vtmp[:, t2 * 512:(t2 + 1) * 512],
                                                                   pa, AF.Copy))
                if t2 == 1:
                    self.emit_vt_transpose(m, self.vtmp, (H["cA"], ca))
                return (H["cA"], ca)
            self.emit_proj(self.wupvb, 8, ckv_rhs, v_evac, ckv_tok)
            def kc_evac(m, t2, pa, w):
                dst = self.kcs_tiles[m][:, t2 * 512:(t2 + 1) * 512]
                ca = self.opi("scalar", w, "cA", lambda e, dst=dst, pa=pa: e.activation(dst, pa, AF.Copy))
                return (H["cA"], ca)
            self.emit_proj(self.wupkb, 4, ckv_rhs, kc_evac, ckv_tok)
            kc_tok = self.tok("cA")
            for kvh in range(NKV):
                pair, par = kvh // 2, kvh % 2
                for t2 in range(2):
                    pa = self.psB[:, t2 * 512:(t2 + 1) * 512]
                    w = [kr_cv, kc_tok, self.psB_tok[t2], tab_tok]
                    def asm(e, pair=pair, par=par, t2=t2, pa=pa):
                        e.matmul(pa, self.t_perm[:, par * 128:(par + 1) * 128],
                                 self.kr0[:, pair * T + t2 * 512: pair * T + (t2 + 1) * 512],
                                 start=True, stop=False)
                        return e.matmul(pa, self.t_perm[:, 256 + par * 128:256 + (par + 1) * 128],
                                        self.kcs_tiles[pair][:, t2 * 512:(t2 + 1) * 512],
                                        start=False, stop=True)
                    pvv = self.opi("tensor", w, "cP", asm)
                    dst = self.kst[:, kvh * S + t2 * 512: kvh * S + (t2 + 1) * 512]
                    ca = self.opi("scalar", [(H["cP"], pvv)], "cA",
                                  lambda e, dst=dst, pa=pa: e.activation(dst, pa, AF.Copy))
                    self.psB_tok[t2] = (H["cA"], ca)

        xch_tok = self.emit_exchange("cc1" if isa else "cc2")

        self.qsp_tok = {0: None, 1: None}
        wq = self.wqa if isa else self.wqb
        def q_evac(m, t2, pa, w):
            par = m & 1
            qt = self.qtm[par]
            ww = list(w) + [self.scal_cv] + ([self.qsp_tok[par]] if t2 == 0 else [])
            ca = self.opi("scalar", ww, "cA",
                          lambda e, t2=t2, pa=pa, qt=qt:
                          e.activation(qt[:, t2 * 512:(t2 + 1) * 512], pa, AF.Copy,
                                       scale=self.scalB[:, 3:4]))
            if t2 == 1:
                cv = self.emit_rope(qt, self.t_cos, self.t_sin, [(H["cA"], ca)])
                da = self.opi("scalar", [(H["cV"], cv)], "dA",
                              lambda e, m=m, qt=qt: e.dma_start(out=self.qsp[m], in_=qt[:]), 16)
                self.qsp_tok[par] = (H["dA"], da)
            return (H["cA"], ca)
        self.emit_proj(wq, 32, lambda c, t2: self.st_chh(c, t2), q_evac, y_tok)
        qsp_done = self.tok("dA")

        kv_ready = [xch_tok, self.tok("cV"), self.tok("cA")]
        self.emit_attn_core(kv_ready, qsp_done)

        da = self.opi("sync", [self.tok("dA")], "dS",
                      lambda e: e.dma_start(
                          out=bass.AP(self.aarena, 0, [[DC * T, 128], [T, DC], [1, T]]),
                          in_=bass.AP(self.asp.tensor, 0, [[T, 128], [128 * T, DC], [1, T]])), 16)
        wo = self.woa if isa else self.wob
        def wo_evac(m, t2, pa, w):
            mu = self.muB[:, t2 * 512:(t2 + 1) * 512]
            sg = self.sgB[:, t2 * 512:(t2 + 1) * 512]
            ych = self.st_chh(m, t2)
            sf = self.scrF[:, 0:TH]
            self.op("vector", w, None, lambda e, ych=ych, sg=sg, sf=sf: e.tensor_mul(sf, ych, sg))
            self.op("vector", [], None, lambda e, pa=pa, sf=sf: e.tensor_add(sf, sf, pa))
            cv = self.opi("vector", [], "cV",
                          lambda e, ych=ych, sf=sf, mu=mu: e.tensor_add(ych, sf, mu))
            return (H["cV"], cv)
        self.emit_proj(wo, 32,
                       lambda c, t2: self.aarena[:, c * T + t2 * 512: c * T + (t2 + 1) * 512],
                       wo_evac, (H["dS"], da))
        return self.tok("cV")

    def emit_attn_core(self, kv_ready, qsp_done):
        H = self.H
        qb_tok = {0: None, 1: None}
        ab_tok = {0: None, 1: None}
        exp_tok = None
        trans_done = None
        first_core = list(kv_ready) + [qsp_done]
        for qg in range(2):
            own_pad = 512 * (qg + 1)
            wp = own_pad + 1024
            mk = self.opi("scalar", [self.tok("cP")], "dA",
                          lambda e, qg=qg: e.dma_start(
                              out=bass.AP(self.mstg, 0, [[4 * 1536, 128], [1536, 4], [1, 1536]]),
                              in_=bass.AP(self.masks.tensor, qg * 4 * 128 * 1536,
                                          [[1536, 128], [128 * 1536, 4], [1, 1536]])), 16)
            for h in range(NH):
                kv = h // 4
                par = h & 1
                qs_ap = self.qbuf[:, par * 512:(par + 1) * 512]
                qv = self.opi("scalar", [qb_tok[par], qsp_done], "dA",
                              lambda e, h=h, qg=qg, qs_ap=qs_ap:
                              e.dma_start(out=qs_ap, in_=self.qsp[h][:, qg * 512:(qg + 1) * 512]), 16)
                for qt in range(4):
                    j = qg * 4 + qt
                    w = [(H["dA"], qv), (H["dA"], mk)]
                    if qg == 0 and h == 0 and qt == 0:
                        w += first_core
                    if exp_tok is not None:
                        w.append(exp_tok)
                    def scr(e, qt=qt, kv=kv, wp=wp, qs_ap=qs_ap):
                        last = None
                        lh = qs_ap[:, qt * 128:(qt + 1) * 128]
                        for s0 in range(0, wp, 512):
                            last = e.matmul(self.psA[:, s0:s0 + 512], lh,
                                            self.kst[:, kv * S + s0: kv * S + s0 + 512],
                                            start=True, stop=True)
                        return last
                    pvv = self.opi("tensor", w, "cP", scr)
                    mw = min(1536, 2048 - 128 * j)
                    self.op("vector", [(H["cP"], pvv)], None,
                            lambda e, j=j, qt=qt, mw=mw: e.tensor_tensor(
                                self.psA[:, 128 * j:128 * j + mw], self.psA[:, 128 * j:128 * j + mw],
                                self.mstg[:, qt * 1536: qt * 1536 + mw], ALU.add))
                    cv = self.opi("vector", [], "cV",
                                  lambda e, wp=wp: e.reduce_max(self.nmx[:, 0:1], self.psA[:, 0:wp],
                                                                axis=AX, negate=True))
                    we = [(H["cV"], cv)]
                    if trans_done is not None:
                        we.append(trans_done)
                    ca = self.opi("scalar", we, "cA",
                                  lambda e, wp=wp: e.activation(self.psb[:, 0:wp], self.psA[:, 0:wp],
                                                                AF.Exp, bias=self.nmx[:, 0:1],
                                                                accum_out=self.nmx[:, 1:2]))
                    exp_tok = (H["cA"], ca)
                    cv = self.opi("vector", [exp_tok], "cV",
                                  lambda e: e.reciprocal(self.nmx[:, 2:3], self.nmx[:, 1:2]))
                    cv = self.opi("vector", [(H["cV"], cv)], "cV",
                                  lambda e, wp=wp: e.tensor_scalar_mul(self.psb[:, 0:wp],
                                                                       self.psb[:, 0:wp], self.nmx[:, 2:3]))
                    ptok = (H["cV"], cv)
                    nch = wp // 128
                    for b0 in range(0, nch, 4):
                        nb = min(4, nch - b0)
                        bsel = (b0 // 4) & 1
                        pt = self.psT[bsel]
                        def trs(e, b0=b0, nb=nb, pt=pt):
                            last = None
                            for k in range(nb):
                                last = e.transpose(pt[:, k * 128:(k + 1) * 128],
                                                   self.psb[:, (b0 + k) * 128:(b0 + k + 1) * 128],
                                                   self.t_id[:])
                            return last
                        pvv = self.opi("tensor", [ptok, self.psT_tok[bsel]], "cP", trs)
                        def ev(e, b0=b0, nb=nb, pt=pt, qt=qt):
                            dst = bass.AP(self.ptt, b0 * 512 + qt * 128,
                                          [[16 * 512, 128], [512, nb], [1, 128]])
                            return e.tensor_copy(dst, pt[:, 0:nb * 128].rearrange("p (c n) -> p c n", c=nb))
                        cvv = self.opi("vector", [(H["cP"], pvv)], "cV", ev)
                        self.psT_tok[bsel] = (H["cV"], cvv)
                    trans_done = self.tok("cP")
                pa = self.psB[:, par * 512:(par + 1) * 512]
                glist = list(range(wp // 128))
                w = [self.tok("cV"), self.psB_tok[par]]
                def pv(e, glist=glist, kv=kv, qg=qg, pa=pa):
                    last = None
                    for gi, g in enumerate(glist):
                        qs = 128 * max(0, (g - 8) - qg * 4) if g >= 8 else 0
                        last = e.matmul(pa[:, qs:512],
                                        self.vtt[:, g * 1024 + kv * 128: g * 1024 + (kv + 1) * 128],
                                        self.ptt[:, g * 512 + qs: (g + 1) * 512],
                                        start=(gi == 0), stop=(gi == len(glist) - 1))
                    return last
                pvv = self.opi("tensor", w, "cP", pv)
                ab = self.abuf[:, par * 512:(par + 1) * 512]
                cv = self.opi("vector", [(H["cP"], pvv), ab_tok[par]], "cV",
                              lambda e, ab=ab, pa=pa: e.tensor_copy(ab, pa))
                self.psB_tok[par] = (H["cV"], cv)
                da = self.opi("scalar", [(H["cV"], cv)], "dA",
                              lambda e, h=h, qg=qg, ab=ab:
                              e.dma_start(out=self.asp[h * 128:(h + 1) * 128, qg * 512:(qg + 1) * 512],
                                          in_=ab), 16)
                ab_tok[par] = (H["dA"], da)
                qb_tok[par] = self.tok("cP")

    def emit_ffn(self, y_tok, use_cache, write_cache, final):
        H = self.H
        pv_toks = [None, None]
        out_dma_tok = None
        dsem = "dS" if use_cache else "dG"
        drec = "sync" if use_cache else "gpsimd"
        for th in range(2):
            for i in range(IC):
                par = i & 1
                gst = self.wsA if par == 0 else self.wsB
                ust = self.usA if par == 0 else self.usB
                gsrc = self.gcc[i] if use_cache else self.gw[i]
                usrc = self.ucc[i] if use_cache else self.uw[i]
                self.opi(drec, [self.ws_tok[par]], dsem,
                         lambda e, gst=gst, gsrc=gsrc: e.dma_start(out=gst[:], in_=gsrc), 16)
                dv = self.opi(drec, [self.us_tok[par]], dsem,
                              lambda e, ust=ust, usrc=usrc: e.dma_start(out=ust[:], in_=usrc), 16)
                if write_cache and th == 0:
                    self.opi("sync", [(H[dsem], dv)], "dS",
                             lambda e, gst=gst, i=i: e.dma_start(out=self.gcc[i], in_=gst[:]), 16)
                    self.opi("sync", [], "dS",
                             lambda e, ust=ust, i=i: e.dma_start(out=self.ucc[i], in_=ust[:]), 16)
                if MK_DUMP == "stage" and th == 0 and i in (0, 1, 2, 3):
                    self.opi("gpsimd", [(H[dsem], dv)], "dG",
                             lambda e, gst=gst, i=i: e.dma_start(
                                 out=self.out[1024 + 256 * i:1024 + 256 * i + 128, 0:1024],
                                 in_=gst[:, 0:1024]), 16)
                    self.opi("gpsimd", [], "dG",
                             lambda e, ust=ust, i=i: e.dma_start(
                                 out=self.out[1152 + 256 * i:1152 + 256 * i + 128, 0:1024],
                                 in_=ust[:, 0:1024]), 16)
                pg = self.pGH[par][:, 0:512]
                pu = self.pGH[par][:, 512:1024]
                w = [(H[dsem], dv), y_tok, pv_toks[par]]
                if i == 0:
                    w.append(self.ffn_prev_down)
                def mms(e, i=i, th=th, gst=gst, ust=ust, pg=pg, pu=pu):
                    for c in range(DC):
                        e.matmul(pg, gst[:, c * 128:(c + 1) * 128], self.st_chh(c, th),
                                 start=(c == 0), stop=(c == DC - 1))
                    last = None
                    for c in range(DC):
                        last = e.matmul(pu, ust[:, c * 128:(c + 1) * 128], self.st_chh(c, th),
                                        start=(c == 0), stop=(c == DC - 1))
                    return last
                pvv = self.opi("tensor", w, "cP", mms)
                self.ws_tok[par] = (H["cP"], pvv)
                self.us_tok[par] = (H["cP"], pvv)
                gsl = self.g_sb[:, par * TH:(par + 1) * TH]
                ca = self.opi("scalar", [(H["cP"], pvv), self.scal_cv], "cA",
                              lambda e, pg=pg, gsl=gsl: e.activation(gsl, pg, AF.Silu,
                                                                     scale=self.scalB[:, 0:1]))
                if MK_DUMP == "act" and th == 0 and i == 0:
                    cvd = self.opi("vector", [(H["cP"], pvv)], "cV",
                                   lambda e, pg=pg: e.tensor_copy(self.scrF[:, 0:TH], pg))
                    self.opi("gpsimd", [(H["cV"], cvd)], "dG",
                             lambda e: e.dma_start(out=self.out[128:256, 0:TH], in_=self.scrF[:, 0:TH]), 16)
                    self.opi("gpsimd", [(H["cA"], ca)], "dG",
                             lambda e, gsl=gsl: e.dma_start(out=self.out[256:384, 0:TH], in_=gsl), 16)
                if MK_DUMP == "act3" and th == 1 and i == 0:
                    cvd = self.opi("vector", [(H["cP"], pvv)], "cV",
                                   lambda e, pg=pg: e.tensor_copy(self.scrF[:, 0:TH], pg))
                    self.opi("gpsimd", [(H["cV"], cvd)], "dG",
                             lambda e: e.dma_start(out=self.out[2176:2304, 0:TH], in_=self.scrF[:, 0:TH]), 16)
                    self.opi("gpsimd", [(H["cA"], ca)], "dG",
                             lambda e, gsl=gsl: e.dma_start(out=self.out[2304:2432, 0:TH], in_=gsl), 16)
                    self.opi("gpsimd", [(H[dsem], dv)], "dG",
                             lambda e, gst=gst: e.dma_start(out=self.out[2432:2560, 0:1024], in_=gst[:, 0:1024]), 16)
                    self.opi("gpsimd", [(H["cV"], cvd)], "dG",
                             lambda e: e.dma_start(out=self.out[2560:2688, 0:TH], in_=self.st_chh(0, 1)), 16)
                if MK_DUMP == "act2" and th == 0 and i == 0:
                    cvd = self.opi("vector", [(H["cP"], pvv)], "cV",
                                   lambda e, pu=pu: e.tensor_copy(self.scrF[:, 0:TH], pu))
                    self.opi("gpsimd", [(H["cV"], cvd)], "dG",
                             lambda e: e.dma_start(out=self.out[384:512, 0:TH], in_=self.scrF[:, 0:TH]), 16)
                    cvd2 = self.opi("vector", [(H["cV"], cvd)], "cV",
                                    lambda e: e.tensor_copy(self.scrF[:, 600:608], self.scalB[:]))
                    self.opi("gpsimd", [(H["cV"], cvd2)], "dG",
                             lambda e: e.dma_start(out=self.out[512:640, 0:8], in_=self.scrF[:, 600:608]), 16)
                    self.opi("gpsimd", [], "dG",
                             lambda e: e.dma_start(out=self.out[640:768, 0:8], in_=self.scalB[:]), 16)
                hd = self.hst[:, i * TH:(i + 1) * TH]
                cv = self.opi("vector", [(H["cA"], ca)], "cV",
                              lambda e, hd=hd, pu=pu, gsl=gsl: e.scalar_tensor_tensor(
                                  hd, pu, self.scalB[:, 1:2], gsl, ALU.mult, ALU.mult))
                pv_toks[par] = (H["cV"], cv)
                if MK_DUMP == "h1" and th == 1 and i < 8:
                    self.opi("gpsimd", [(H["cV"], cv)], "dG",
                             lambda e, i=i, hd=hd: e.dma_start(
                                 out=self.out[i * 128:(i + 1) * 128, 512:1024], in_=hd), 16)
                if MK_DUMP == "h" and th == 0 and i < 8:
                    self.opi("gpsimd", [(H["cV"], cv)], "dG",
                             lambda e, i=i, hd=hd: e.dma_start(
                                 out=self.out[i * 128:(i + 1) * 128, 0:TH], in_=hd), 16)
            gu_done = self.tok("cV")
            ev_toks = [None, None]
            for g in range(8):
                pslot = g & 1
                px = self.pGH[pslot]
                mm_first = True
                for ib0 in range(0, IC, 8):
                    sz = min(8, IC - ib0)
                    par = (ib0 // 8) & 1
                    dstg = self.wsA if par == 0 else self.wsB
                    dten = self.dcc.tensor if use_cache else self.dwt.tensor
                    doff = (g * IC + ib0) * 128 * 512
                    dsrc = bass.AP(dten, doff, [[512, 128], [128 * 512, sz], [1, 512]])
                    ddst = bass.AP(dstg, 0, [[4096, 128], [512, sz], [1, 512]])
                    dv = self.opi(drec, [self.ws_tok[par], gu_done if (g == 0 and ib0 == 0) else None],
                                  dsem,
                                  lambda e, ddst=ddst, dsrc=dsrc:
                                  e.dma_start(out=ddst, in_=dsrc), 16)
                    if write_cache and th == 0:
                        csrc = bass.AP(self.dcc.tensor, doff, [[512, 128], [128 * 512, sz], [1, 512]])
                        self.opi("sync", [(H[dsem], dv)], "dS",
                                 lambda e, ddst=ddst, csrc=csrc:
                                 e.dma_start(out=csrc, in_=ddst), 16)
                    w = [(H[dsem], dv), gu_done]
                    if mm_first:
                        w.append(ev_toks[pslot])
                        if g < 2:
                            w.append(pv_toks[pslot])
                    def dmm(e, ib0=ib0, sz=sz, dstg=dstg, px=px, mm_first=mm_first, th=th):
                        last = None
                        for ii in range(sz):
                            i = ib0 + ii
                            for ncl in range(4):
                                last = e.matmul(px[:, ncl * 512:(ncl + 1) * 512],
                                                dstg[:, ii * 512 + ncl * 128: ii * 512 + (ncl + 1) * 128],
                                                self.hst[:, i * TH:(i + 1) * TH],
                                                start=(mm_first and ii == 0),
                                                stop=(ib0 + sz == IC and ii == sz - 1))
                        return last
                    pvv = self.opi("tensor", w, "cP", dmm)
                    self.ws_tok[par] = (H["cP"], pvv)
                    mm_first = False
                dn_tok = (H["cP"], self.cnt["cP"])
                if MK_DUMP == "down" and th == 0 and g == 0:
                    cvd = self.opi("vector", [dn_tok], "cV",
                                   lambda e, px=px: e.tensor_copy(self.scrF[:, 0:TH], px[:, 0:512]))
                    self.opi("gpsimd", [(H["cV"], cvd)], "dG",
                             lambda e: e.dma_start(out=self.out[2048:2176, 0:TH], in_=self.scrF[:, 0:TH]), 16)
                for ncl in range(4):
                    m = g * 4 + ncl
                    mu = self.muB[:, th * 512:(th + 1) * 512]
                    sg = self.sgB[:, th * 512:(th + 1) * 512]
                    ych = self.st_chh(m, th)
                    sf = self.scrF[:, 0:TH]
                    sfo = self.scrF[:, 512:1024]
                    pa = px[:, ncl * 512:(ncl + 1) * 512]
                    self.op("vector", [dn_tok] + ([out_dma_tok] if final else []), None,
                            lambda e, ych=ych, sg=sg, sf=sf: e.tensor_mul(sf, ych, sg))
                    self.op("vector", [self.scal_cv], None,
                            lambda e, pa=pa, sf=sf: e.scalar_tensor_tensor(
                                sf, pa, self.scalB[:, 2:3], sf, ALU.mult, ALU.add))
                    if final:
                        cv = self.opi("vector", [], "cV",
                                      lambda e, sf=sf, sfo=sfo, mu=mu: e.tensor_add(sfo, sf, mu))
                        da = self.opi("sync", [(H["cV"], cv)], "dS",
                                      lambda e, m=m, th=th, sfo=sfo: e.dma_start(
                                          out=self.out[m * 128:(m + 1) * 128, th * 512:(th + 1) * 512],
                                          in_=sfo), 16)
                        out_dma_tok = (H["dS"], da)
                    else:
                        cv = self.opi("vector", [], "cV",
                                      lambda e, ych=ych, sf=sf, mu=mu: e.tensor_add(ych, sf, mu))
                ev_toks[pslot] = (H["cV"], self.cnt["cV"])
            pv_toks = [self.tok("cV"), self.tok("cV")]
            self.ffn_prev_down = self.tok("cP")
            self.emit_barrier()
        return self.tok("cV")

    def emit_barrier(self):
        H = self.H
        self.opi("tensor", [], "cP", lambda e: e.nop())
        self.opi("vector", [], "cV", lambda e: e.nop())
        self.opi("scalar", [], "cA", lambda e: e.nop())
        w = [self.tok("cP"), self.tok("cV"), self.tok("cA"),
             self.tok("dS"), self.tok("dG"), self.tok("dA")]
        for eng in ("tensor", "vector", "scalar", "gpsimd", "sync"):
            self.op(eng, w, None, lambda e: None)
        self.opi("tensor", [], "cP", lambda e: e.nop())
        self.opi("vector", [], "cV", lambda e: e.nop())
        self.opi("scalar", [], "cA", lambda e: e.nop())

    def emit_dump_stash(self):
        w = [self.tok("cV")]
        for c in range(DC):
            self.opi("gpsimd", w if c == 0 else [], "dG",
                     lambda e, c=c: e.dma_start(out=self.out[c * 128:(c + 1) * 128, :], in_=self.st_ch(c)), 16)
        self.op("sync", [self.tok("dG")], None, lambda e: None)

    def build(self):
        self.setup()
        t1 = self.emit_ln(self.rhoA, first=True)
        t1 = self.emit_attn("a", t1)
        if self.n_sub >= 2:
            t1 = self.emit_ln(self.rhoF)
            if MK_DUMP == "y2":
                self.emit_dump_stash()
                return self._finish()
            t1 = self.emit_ffn(t1, use_cache=False, write_cache=(self.n_sub >= 4), final=False)
            if MK_DUMP in ("h", "h1", "gate", "stage", "act", "act2", "act3", "down"):
                self.op("sync", [self.tok("dG")], None, lambda e: None)
                return self._finish()
        if self.n_sub >= 3:
            t1 = self.emit_ln(self.rhoA)
            t1 = self.emit_attn("b", t1)
        if self.n_sub >= 4:
            t1 = self.emit_ln(self.rhoF)
            t1 = self.emit_ffn(t1, use_cache=True, write_cache=False, final=True)
            self.op("sync", [self.tok("dS")], None, lambda e: None)
        else:
            self.emit_dump_stash()

        return self._finish()

    def _finish(self):
        with self.nc.Block() as block:
            for name, reg in (("tensor", block.tensor), ("vector", block.vector),
                              ("scalar", block.scalar), ("gpsimd", block.gpsimd),
                              ("sync", block.sync)):
                ops = self.ops[name]
                def runner(e, ops=ops):
                    for i0 in range(0, len(ops), CHUNK_OPS):
                        with e.Fori(0, 1):
                            for f in ops[i0:i0 + CHUNK_OPS]:
                                f(e)
                reg(runner)
        return self.nc


def _tile_lhsT(W):
    K, M = W.shape
    Kc, Mc = K // 128, M // 128
    return np.ascontiguousarray(W.reshape(Kc, 128, Mc, 128).transpose(2, 1, 0, 3))


def _bf(x):
    return np.asarray(x).astype(ml_dtypes.bfloat16)


def _host_prep(inputs):
    f32 = np.float32
    WqA = np.asarray(inputs["L0_attn_a_Wq"], f32)
    WkA = np.asarray(inputs["L0_attn_a_Wk"], f32)
    WvA = np.asarray(inputs["L0_attn_a_Wv"], f32)
    WoA = np.asarray(inputs["L0_attn_a_Wo"], f32)
    WqB = np.asarray(inputs["L0_attn_b_Wq"], f32)
    WkrB = np.asarray(inputs["L0_attn_b_Wkr"], f32)
    WdkB = np.asarray(inputs["L0_attn_b_Wdk"], f32)
    WupkB = np.asarray(inputs["L0_attn_b_Wupk"], f32)
    WupvB = np.asarray(inputs["L0_attn_b_Wupv"], f32)
    WoB = np.asarray(inputs["L0_attn_b_Wo"], f32)
    gwN = np.asarray(inputs["L0_ffn_gate_w"], np.int32)
    uwN = np.asarray(inputs["L0_ffn_up_w"], np.int32)
    dwN = np.asarray(inputs["L0_ffn_down_w"], np.int32)

    p128 = np.empty(128, np.int64)
    p128[0:64] = np.arange(0, 128, 2)
    p128[64:128] = np.arange(1, 128, 2)
    qperm = np.concatenate([h * 128 + p128 for h in range(NH)])
    kperm = np.concatenate([h * 128 + p128 for h in range(NKV)])
    krp = []
    for pr in range(4):
        h0, h1 = 2 * pr, 2 * pr + 1
        krp += list(range(64 * h0, 64 * h0 + 32)) + list(range(64 * h1, 64 * h1 + 32))
        krp += list(range(64 * h0 + 32, 64 * h0 + 64)) + list(range(64 * h1 + 32, 64 * h1 + 64))
    krp = np.array(krp)

    pre = {
        "wqa": _tile_lhsT(WqA[:, qperm]),
        "wka": _tile_lhsT(WkA[:, kperm]),
        "wva": _tile_lhsT(WvA),
        "woa": _tile_lhsT(WoA),
        "wqb": _tile_lhsT(np.ascontiguousarray(WqB.T)),
        "wkrb": _tile_lhsT(np.ascontiguousarray(WkrB.T)[:, krp]),
        "wdkb": _tile_lhsT(np.ascontiguousarray(WdkB.T)),
        "wupkb": _tile_lhsT(np.ascontiguousarray(WupkB.T)),
        "wupvb": _tile_lhsT(np.ascontiguousarray(WupvB.T)),
        "wob": _tile_lhsT(np.ascontiguousarray(WoB.T)),
        "gw": np.ascontiguousarray(gwN.reshape(86, 128, 32, 128).transpose(0, 3, 2, 1)),
        "uw": np.ascontiguousarray(uwN.reshape(86, 128, 32, 128).transpose(0, 3, 2, 1)),
        "dwt": np.ascontiguousarray(dwN.reshape(8, 512, 86, 128).transpose(0, 2, 3, 1)),
    }

    PRe = np.zeros((128, 128), f32)
    PRo = np.zeros((128, 128), f32)
    PNe = np.zeros((128, 128), f32)
    PNo = np.zeros((128, 128), f32)
    for i in range(32):
        PRe[i, i] = 1; PRe[64 + i, 64 + i] = 1
        PRo[32 + i, i] = 1; PRo[96 + i, 64 + i] = 1
        PNe[i, 32 + i] = 1; PNe[32 + i, 96 + i] = 1
        PNo[64 + i, 32 + i] = 1; PNo[96 + i, 96 + i] = 1
    pre["permm"] = _bf(np.concatenate([PRe, PRo, PNe, PNo], axis=1))
    pre["identi"] = _bf(np.eye(128, dtype=f32))
    ob = np.zeros((128, 2), f32)
    ob[:, 0] = 1.0
    pre["onesb"] = _bf(ob)
    pre["onesf"] = np.ones((1, 128), f32)
    pre["consts"] = np.array([[float(np.asarray(inputs["L0_ffn_gate_s"]).ravel()[0]),
                               float(np.asarray(inputs["L0_ffn_up_s"]).ravel()[0]),
                               float(np.asarray(inputs["L0_ffn_down_s"]).ravel()[0]),
                               INV_SQRT_D, EPS, 0.0, 0.0, 0.0]], f32)
    return pre


def _core_prep(inputs, core):
    f32 = np.float32
    b, hf = core // 2, core % 2
    x = np.asarray(inputs["L0_attn_a_x"], f32)[b]
    xin = np.ascontiguousarray(x[hf * T:(hf + 1) * T, :].T)

    pos = hf * T + np.arange(T, dtype=np.float64)
    inv = 10000.0 ** (-np.arange(64, dtype=np.float64) / 64.0)
    ridx = np.arange(128) % 64
    ang = pos[None, :] * inv[ridx][:, None]
    sgn = np.where(np.arange(128) < 64, 1.0, -1.0)[:, None]
    ropa = np.stack([np.cos(ang), sgn * np.sin(ang)]).astype(f32)
    ridx2 = np.arange(128) % 32
    ang2 = pos[None, :] * inv[ridx2][:, None]
    ropk = np.stack([np.cos(ang2), sgn * np.sin(ang2)]).astype(f32)

    m = np.zeros((8, 128, 1536), f32)
    for j in range(8):
        wp = 1024 + 512 * (j // 4 + 1)
        qg = hf * T + 128 * j + np.arange(128)
        kg = 128 * j + np.arange(1536)
        vis = kg[None, :] <= qg[:, None]
        m[j] = np.where(vis, 0.0, NEG)
        if wp - 128 * j < 1536:
            m[j][:, wp - 128 * j:] = 0.0
    return {"xin": xin, "ropa": _bf(ropa), "ropk": _bf(ropk), "masks": _bf(m)}


def kernel(**inputs):
    pre = _host_prep(inputs)
    in_maps = []
    for c in range(8):
        d = dict(pre)
        d.update(_core_prep(inputs, c))
        in_maps.append(d)
    mk = MK(N_SUB)
    nc = mk.build()
    res = run_bass_kernel_spmd(nc, in_maps, core_ids=list(range(8)))
    out = np.empty((4, S, D), np.float32)
    for c in range(8):
        b, hf = c // 2, c % 2
        out[b, hf * T:(hf + 1) * T, :] = np.asarray(res.results[c]["out"]).T
    return out


# version 02
# speedup vs baseline: 1.2067x; 1.2067x over previous
import os
import numpy as np
import ml_dtypes
import concourse.bass as bass
import concourse.mybir as mybir
from concourse.bass_utils import run_bass_kernel_spmd

F32 = mybir.dt.float32
BF16 = mybir.dt.bfloat16
I32 = mybir.dt.int32
AF = mybir.ActivationFunctionType
ALU = mybir.AluOpType
AX = mybir.AxisListType.X

D = 4096
DC = 32
T = 1024
TH = 512
S = 2048
IC = 86
NH = 32
NKV = 8
INV_SQRT_D = float(1.0 / np.sqrt(128.0))
EPS = 1e-5
NEG = -1e9

N_SUB = int(os.environ.get("MK_NSUB", "4"))
MK_DUMP = os.environ.get("MK_DUMP", "")
MK_LOCKSTEP = os.environ.get("MK_LOCKSTEP", "F")
CHUNK_OPS = 400


class MK:
    def __init__(self, n_sub=4):
        self.n_sub = n_sub
        nc = bass.Bass("TRN2", target_bir_lowering=False)
        self.nc = nc
        self.ops = {k: [] for k in ("tensor", "vector", "scalar", "gpsimd", "sync")}
        self._ctx = []

        din = lambda n, s, d: nc.dram_tensor(n, s, d, kind="ExternalInput").ap()
        self.xin = din("xin", [D, T], F32)
        self.wqa = din("wqa", [32, 128, 32, 128], F32)
        self.wka = din("wka", [8, 128, 32, 128], F32)
        self.wva = din("wva", [8, 128, 32, 128], F32)
        self.woa = din("woa", [32, 128, 32, 128], F32)
        self.wqb = din("wqb", [32, 128, 32, 128], F32)
        self.wkrb = din("wkrb", [4, 128, 32, 128], F32)
        self.wdkb = din("wdkb", [4, 128, 32, 128], F32)
        self.wupkb = din("wupkb", [4, 128, 4, 128], F32)
        self.wupvb = din("wupvb", [8, 128, 4, 128], F32)
        self.wob = din("wob", [32, 128, 32, 128], F32)
        self.gw = din("gw", [86, 128, 32, 128], I32)
        self.uw = din("uw", [86, 128, 32, 128], I32)
        self.dwt = din("dwt", [8, 86, 128, 512], I32)
        self.masks = din("masks", [8, 128, 1536], BF16)
        self.ropa = din("ropa", [2, 128, T], BF16)
        self.ropk = din("ropk", [2, 128, T], BF16)
        self.permm = din("permm", [128, 512], BF16)
        self.identi = din("identi", [128, 128], BF16)
        self.onesb = din("onesb", [128, 2], BF16)
        self.onesf = din("onesf", [1, 128], F32)
        self.consts = din("consts", [1, 8], F32)
        self.out = nc.dram_tensor("out", [D, T], F32, kind="ExternalOutput").ap()

        self.qsp = nc.dram_tensor("qsp", [32, 128, T], BF16).ap()
        self.asp = nc.dram_tensor("asp", [D, T], BF16).ap()
        self.ccin_t = nc.dram_tensor("ccin", [2048, T], BF16)
        self.ccout_t = nc.dram_tensor("ccout", [4096, T], BF16)
        self.gcc = nc.dram_tensor("gcc", [86, 128, 32, 128], BF16).ap()
        self.ucc = nc.dram_tensor("ucc", [86, 128, 32, 128], BF16).ap()
        self.dcc = nc.dram_tensor("dcc", [8, 86, 128, 512], BF16).ap()

        self.alloc_map = {}

        def sb(name, shape, dt):
            b0 = nc.sbuf_base
            cm = nc.sbuf_tensor(name, shape, dt)
            h = cm.__enter__()
            self._ctx.append(cm)
            self.alloc_map[name] = (b0, nc.sbuf_base)
            return h

        def psmm(name, shape, dt):
            cm = nc.psum_tensor(name, shape, dt)
            h = cm.__enter__()
            self._ctx.append(cm)
            return h

        self.stash = sb("stash", [128, DC * T], BF16)
        self.muB = sb("muB", [128, T], F32)
        self.sgB = sb("sgB", [128, T], F32)
        self.scrF = sb("scrF", [128, T], F32)
        self.xsq = sb("xsq", [128, 2 * T], BF16)
        self.row = sb("row", [1, T], F32)
        self.t_id = sb("t_id", [128, 128], BF16)
        self.t_ones_b = sb("t_ones_b", [128, 2], BF16)
        self.t_ones_f = sb("t_ones_f", [1, 128], F32)
        self.t_consts = sb("t_consts", [1, 8], F32)
        self.scalB = sb("scalB", [128, 8], F32)
        self.nmx = sb("nmx", [128, 4], F32)
        base_before_us = None
        self.wsA = sb("wsA", [128, 4096], BF16)
        self.wsB = sb("wsB", [128, 4096], BF16)
        base_before_us = nc.sbuf_base
        self.usA = sb("usA", [128, 4096], BF16)
        self.usB = sb("usB", [128, 4096], BF16)
        base_after_us = nc.sbuf_base

        nc.sbuf_base = base_before_us
        self.kst = sb("kst", [128, NKV * S], BF16)
        self.vtt = sb("vtt", [128, 16 * 1024], BF16)
        self.rhoA = sb("rhoA", [128, T], F32)
        sub_base = nc.sbuf_base
        self.t_cos = sb("t_cos", [128, T], BF16)
        self.t_sin = sb("t_sin", [128, T], BF16)
        self.t_cos2 = sb("t_cos2", [128, T], BF16)
        self.t_sin2 = sb("t_sin2", [128, T], BF16)
        self.t_perm = sb("t_perm", [128, 512], BF16)
        self.kr0 = sb("kr0", [128, 4 * T], BF16)
        self.ckv = sb("ckv", [128, 4 * T], BF16)
        self.vtmp = sb("vtmp", [128, T], BF16)
        self.qtm = [sb("qtmA", [128, T], BF16), sb("qtmB", [128, T], BF16)]
        self.tr1 = sb("tr1", [128, T], BF16)
        self.tr2 = sb("tr2", [128, T], BF16)
        self.kcs_tiles = [self.tr1, self.tr2, self.qtm[0], self.qtm[1]]
        qkv_end = nc.sbuf_base
        nc.sbuf_base = sub_base
        self.ptt = sb("ptt", [128, 16 * 512], BF16)
        self.psb = sb("psb", [128, 2048], BF16)
        self.mstg = sb("mstg", [128, 4 * 1536], BF16)
        self.qbuf = sb("qbuf", [128, 2 * 512], BF16)
        self.abuf = sb("abuf", [128, 1024], BF16)
        core_end = nc.sbuf_base
        attn_end = max(qkv_end, core_end)

        nc.sbuf_base = base_before_us
        self.aarena = sb("aarena", [128, DC * T], BF16)
        self.rhoW = sb("rhoW", [128, T], F32)
        wo_end = nc.sbuf_base

        nc.sbuf_base = base_after_us
        self.rhoF = sb("rhoF", [128, T], F32)
        nc.sbuf_base = base_after_us
        self.hst = sb("hst", [128, IC * TH], BF16)
        ffn_end = nc.sbuf_base
        self.g_sb = self.xsq

        nc.sbuf_base = max(attn_end, wo_end, ffn_end)
        assert nc.sbuf_base <= nc.sbuf_top, f"SBUF over budget: {nc.sbuf_base} > {nc.sbuf_top}"
        self.sbuf_used = nc.sbuf_base

        psum_base0 = nc.psum_base
        self.psA = psmm("psA", [128, 2048], F32)
        self.psB = psmm("psB", [128, 1024], F32)
        self.psT = [psmm("psT0", [128, 1024], BF16), psmm("psT1", [128, 1024], BF16)]
        nc.psum_base = psum_base0
        self.pGH = [psmm("pG", [128, 2048], F32), psmm("pH", [128, 2048], F32)]

        self.H = {}
        self.cnt = {}
        for nm in ("dS", "dG", "dA", "cP", "cV", "cA", "cc1", "cc2"):
            cm = nc.semaphore(nm)
            self.H[nm] = cm.__enter__()
            self._ctx.append(cm)
            self.cnt[nm] = 0

        self.ws_par = 0
        self.ws_tok = {0: None, 1: None}
        self.us_tok = {0: None, 1: None}
        self.psT_tok = [None, None]
        self.psB_tok = [None, None]
        self.vt_last_cP = None
        self.ffn_prev_down = None

    def op(self, eng, waits, inc, body):
        ls = MK_LOCKSTEP
        add = False
        if ls == "1":
            add = True
        elif ls.startswith("F") and getattr(self, "_in_ffn", False):
            if ls == "F":
                add = True
            elif ls == "FP" and eng == "tensor":
                add = True
            elif ls == "FA" and eng == "scalar":
                add = True
            elif ls == "FV" and eng == "vector":
                add = True
            elif ls == "FG" and eng in ("gpsimd", "sync"):
                add = True
            elif ls == "FD" and eng in ("tensor", "vector", "scalar"):
                add = True
            elif ls == "FAV" and eng in ("vector", "scalar"):
                add = True
        if add:
            waits = list(waits) + [self.tok(nm) for nm in ("cP", "cV", "cA")]
        waits = tuple(w for w in waits if w is not None)
        def f(e, waits=waits, inc=inc, body=body):
            for (sh, v) in waits:
                e.wait_ge(sh, v)
            ins = body(e)
            if inc is not None and ins is not None:
                ins.then_inc(inc[0], inc[1])
        self.ops[eng].append(f)

    def opi(self, eng, waits, nm, body, k=1):
        self.op(eng, waits, (self.H[nm], k), body)
        self.cnt[nm] += k
        return self.cnt[nm]

    def tok(self, nm):
        return (self.H[nm], self.cnt[nm])

    def st_ch(self, c):
        return self.stash[:, c * T:(c + 1) * T]

    def st_chh(self, c, th):
        return self.stash[:, c * T + th * TH: c * T + (th + 1) * TH]

    def setup(self):
        for dst, src in ((self.t_id[:], self.identi), (self.t_ones_b[:], self.onesb),
                         (self.t_ones_f[:], self.onesf), (self.t_consts[:], self.consts)):
            self.opi("sync", [], "dS", lambda e, d=dst, s=src: e.dma_start(out=d, in_=s), 16)
        v = self.tok("dS")
        self.opi("tensor", [v], "cP",
                 lambda e: e.matmul(self.psB[:, 0:8], self.t_ones_f[:1, :], self.t_consts[:1, :],
                                    start=True, stop=True))
        self.opi("vector", [self.tok("cP")], "cV",
                 lambda e: e.tensor_copy(self.scalB[:], self.psB[:, 0:8]))
        self.scal_cv = self.tok("cV")

    def emit_ln(self, rho, first=False):
        H = self.H
        if first:
            for c in range(DC):
                self.opi("gpsimd", [], "dG",
                         lambda e, c=c: e.dma_start(out=self.st_ch(c), in_=self.xin[c * 128:(c + 1) * 128, :]), 16)
            in_tok = self.tok("dG")
        else:
            in_tok = self.tok("cV")
        stats_cp = {}
        for c in range(DC):
            sl = (c & 1) * T
            w = [in_tok] if c == 0 else []
            if c >= 2:
                w.append((H["cP"], stats_cp[c - 2]))
            ca = self.opi("scalar", w, "cA",
                          lambda e, c=c, sl=sl: e.activation(self.xsq[:, sl:sl + T], self.st_ch(c), AF.Square))
            def stats(e, c=c, sl=sl):
                last = None
                for t2 in range(2):
                    last = e.matmul(self.psA[0:1, t2 * 512:(t2 + 1) * 512], self.t_ones_b[:, 0:1],
                                    self.st_chh(c, t2), start=(c == 0), stop=(c == DC - 1))
                for t2 in range(2):
                    last = e.matmul(self.psB[0:1, t2 * 512:(t2 + 1) * 512], self.t_ones_b[:, 0:1],
                                    self.xsq[:, sl + t2 * 512: sl + (t2 + 1) * 512],
                                    start=(c == 0), stop=(c == DC - 1))
                return last
            stats_cp[c] = self.opi("tensor", [(H["cA"], ca)], "cP", stats)
        self.opi("vector", [self.tok("cP")], "cV",
                 lambda e: e.tensor_copy(self.row[:], self.psA[0:1, 0:1024]))
        def bsum(e):
            e.matmul(self.psA[:, 1024:1536], self.t_ones_f[:1, :], self.row[:1, 0:512], start=True, stop=True)
            return e.matmul(self.psA[:, 1536:2048], self.t_ones_f[:1, :], self.row[:1, 512:1024],
                            start=True, stop=True)
        self.opi("tensor", [self.tok("cV")], "cP", bsum)
        self.opi("vector", [self.tok("cP")], "cV",
                 lambda e: e.tensor_scalar_mul(self.muB[:], self.psA[:, 1024:2048], 1.0 / D))
        self.opi("vector", [], "cV", lambda e: e.tensor_copy(self.row[:], self.psB[0:1, 0:1024]))
        def bsq(e):
            e.matmul(self.psB[:, 0:512], self.t_ones_f[:1, :], self.row[:1, 0:512], start=True, stop=True)
            return e.matmul(self.psB[:, 512:1024], self.t_ones_f[:1, :], self.row[:1, 512:1024],
                            start=True, stop=True)
        self.opi("tensor", [self.tok("cV")], "cP", bsq)
        self.opi("vector", [], "cV", lambda e: e.tensor_mul(self.scrF[:], self.muB[:], self.muB[:]))
        self.opi("vector", [self.tok("cP")], "cV",
                 lambda e, rho=rho: e.scalar_tensor_tensor(rho[:], self.psB[:], 1.0 / D, self.scrF[:],
                                                           ALU.mult, ALU.subtract))
        self.opi("scalar", [self.tok("cV"), self.scal_cv], "cA",
                 lambda e, rho=rho: e.activation(self.sgB[:], rho[:], AF.Sqrt, bias=self.scalB[:, 4:5]))
        self.opi("vector", [self.tok("cA")], "cV", lambda e, rho=rho: e.reciprocal(rho[:], self.sgB[:]))
        rv = self.tok("cV")
        for c in range(DC):
            self.op("vector", [rv] if c == 0 else [], None,
                    lambda e, c=c: e.tensor_sub(self.scrF[:], self.st_ch(c), self.muB[:]))
            self.opi("vector", [], "cV",
                     lambda e, c=c, rho=rho: e.tensor_mul(self.st_ch(c), self.scrF[:], rho[:]))
        return self.tok("cV")

    def emit_proj(self, wdram, n_chunks, rhs_fn, evac_fn, y_tok, cast=True):
        H = self.H
        kc = wdram.shape[2]
        for m in range(n_chunks):
            par = self.ws_par & 1
            self.ws_par += 1
            stg = self.wsA if par == 0 else self.wsB
            dsem = "dG" if cast else "dS"
            dv = self.opi("gpsimd" if cast else "sync", [self.ws_tok[par]], dsem,
                          lambda e, m=m, stg=stg, kc=kc, wdram=wdram:
                          e.dma_start(out=stg[:, 0:kc * 128], in_=wdram[m]), 16)
            for t2 in range(2):
                pa = self.psB[:, t2 * 512:(t2 + 1) * 512]
                w = [(H[dsem], dv), y_tok, self.psB_tok[t2]]
                def mms(e, m=m, t2=t2, stg=stg, pa=pa, kc=kc, rhs_fn=rhs_fn):
                    last = None
                    for c in range(kc):
                        last = e.matmul(pa, stg[:, c * 128:(c + 1) * 128], rhs_fn(c, t2),
                                        start=(c == 0), stop=(c == kc - 1))
                    return last
                pvv = self.opi("tensor", w, "cP", mms)
                if t2 == 1:
                    self.ws_tok[par] = (H["cP"], pvv)
                self.psB_tok[t2] = evac_fn(m, t2, pa, [(H["cP"], pvv)])

    def emit_rope(self, dst, cosT, sinT, waits):
        self.op("vector", waits, None,
                lambda e, dst=dst: e.tensor_mul(self.tr1[0:64, :], dst[64:128, :], sinT[64:128, :]))
        self.op("vector", [], None,
                lambda e, dst=dst: e.tensor_mul(self.tr1[64:128, :], dst[0:64, :], sinT[0:64, :]))
        self.op("vector", [], None, lambda e, dst=dst: e.tensor_mul(self.tr2[:], dst[:], cosT[:]))
        return self.opi("vector", [], "cV",
                        lambda e, dst=dst: e.tensor_add(dst[:], self.tr2[:], self.tr1[:]))

    def emit_vt_transpose(self, m, src, ca_tok):
        H = self.H
        for b4 in range(2):
            pt = self.psT[b4]
            def trs(e, b4=b4, pt=pt, src=src):
                last = None
                for k in range(4):
                    tc = b4 * 4 + k
                    last = e.transpose(pt[:, k * 128:(k + 1) * 128], src[:, tc * 128:(tc + 1) * 128],
                                       self.t_id[:])
                return last
            pvv = self.opi("tensor", [ca_tok, self.psT_tok[b4]], "cP", trs)
            self.vt_last_cP = (H["cP"], pvv)
            def ev(e, m=m, b4=b4, pt=pt):
                dst = bass.AP(self.vtt, (b4 * 4) * 1024 + m * 128,
                              [[16 * 1024, 128], [1024, 4], [1, 128]])
                return e.tensor_copy(dst, pt[:, 0:512].rearrange("p (c n) -> p c n", c=4))
            cvv = self.opi("vector", [(H["cP"], pvv)], "cV", ev)
            self.psT_tok[b4] = (H["cV"], cvv)

    def emit_exchange(self, ccsem):
        H = self.H
        ccin = self.ccin_t.ap()
        ccout = self.ccout_t.ap()
        w0 = [self.tok("cV"), self.tok("cA")]
        for hc in range(NKV):
            self.opi("gpsimd", w0 if hc == 0 else [], "dG",
                     lambda e, hc=hc, ccin=ccin: e.dma_start(
                         out=ccin[hc * 128:(hc + 1) * 128, :], in_=self.kst[:, hc * S:hc * S + T]), 16)
        for tc in range(8):
            self.opi("gpsimd", [], "dG",
                     lambda e, tc=tc, ccin=ccin: e.dma_start(
                         out=ccin[1024 + tc * 128:1024 + (tc + 1) * 128, :],
                         in_=self.vtt[:, tc * 1024:(tc + 1) * 1024]), 16)
        dv = self.tok("dG")
        def coll(e):
            return e.collective_compute(
                "AllGather", ALU.bypass,
                replica_groups=[[0, 1], [2, 3], [4, 5], [6, 7]],
                ins=[self.ccin_t.ap()], outs=[self.ccout_t.ap()])
        self.op("gpsimd", [dv], (H[ccsem], 1), coll)
        self.cnt[ccsem] += 1
        for r in range(2):
            for hc in range(NKV):
                self.opi("gpsimd", [(H[ccsem], 1)] if (r == 0 and hc == 0) else [], "dG",
                         lambda e, r=r, hc=hc, ccout=ccout: e.dma_start(
                             out=self.kst[:, hc * S + r * T: hc * S + (r + 1) * T],
                             in_=ccout[2048 * r + hc * 128: 2048 * r + (hc + 1) * 128, :]), 16)
            for tc in range(8):
                self.opi("gpsimd", [], "dG",
                         lambda e, r=r, tc=tc, ccout=ccout: e.dma_start(
                             out=self.vtt[:, (8 * r + tc) * 1024:(8 * r + tc + 1) * 1024],
                             in_=ccout[2048 * r + 1024 + tc * 128: 2048 * r + 1024 + (tc + 1) * 128, :]), 16)
        return self.tok("dG")

    def emit_attn(self, layer, y_tok):
        H = self.H
        isa = layer == "a"
        tabs = [(self.t_cos[:], self.ropa[0]), (self.t_sin[:], self.ropa[1])]
        if not isa:
            tabs += [(self.t_cos2[:], self.ropk[0]), (self.t_sin2[:], self.ropk[1]),
                     (self.t_perm[:], self.permm)]
        for dst, src in tabs:
            self.opi("sync", [], "dS", lambda e, d=dst, s=src: e.dma_start(out=d, in_=s), 16)
        tab_tok = self.tok("dS")

        if isa:
            def k_evac(m, t2, pa, w):
                dst = self.kst[:, m * S + t2 * 512: m * S + (t2 + 1) * 512]
                ca = self.opi("scalar", w, "cA", lambda e, dst=dst, pa=pa: e.activation(dst, pa, AF.Copy))
                return (H["cA"], ca)
            self.emit_proj(self.wka, 8, lambda c, t2: self.st_chh(c, t2), k_evac, y_tok)
            first = [tab_tok, self.tok("cA")]
            for m in range(NKV):
                self.emit_rope(self.kst[:, m * S:m * S + T], self.t_cos, self.t_sin, first)
                first = []
            def v_evac(m, t2, pa, w):
                ww = list(w) + ([self.vt_last_cP] if t2 == 0 else [])
                ca = self.opi("scalar", ww, "cA",
                              lambda e, t2=t2, pa=pa: e.activation(self.vtmp[:, t2 * 512:(t2 + 1) * 512],
                                                                   pa, AF.Copy))
                if t2 == 1:
                    self.emit_vt_transpose(m, self.vtmp, (H["cA"], ca))
                return (H["cA"], ca)
            self.emit_proj(self.wva, 8, lambda c, t2: self.st_chh(c, t2), v_evac, y_tok)
        else:
            def ckv_evac(m, t2, pa, w):
                dst = self.ckv[:, m * T + t2 * 512: m * T + (t2 + 1) * 512]
                ca = self.opi("scalar", w, "cA", lambda e, dst=dst, pa=pa: e.activation(dst, pa, AF.Copy))
                return (H["cA"], ca)
            self.emit_proj(self.wdkb, 4, lambda c, t2: self.st_chh(c, t2), ckv_evac, y_tok)
            ckv_tok = self.tok("cA")
            ckv_rhs = lambda c, t2: self.ckv[:, c * T + t2 * 512: c * T + (t2 + 1) * 512]
            def kr_evac(m, t2, pa, w):
                dst = self.kr0[:, m * T + t2 * 512: m * T + (t2 + 1) * 512]
                ca = self.opi("scalar", w, "cA", lambda e, dst=dst, pa=pa: e.activation(dst, pa, AF.Copy))
                return (H["cA"], ca)
            self.emit_proj(self.wkrb, 4, lambda c, t2: self.st_chh(c, t2), kr_evac, y_tok)
            first = [tab_tok, self.tok("cA")]
            for m in range(4):
                self.emit_rope(self.kr0[:, m * T:(m + 1) * T], self.t_cos2, self.t_sin2, first)
                first = []
            kr_cv = self.tok("cV")
            def v_evac(m, t2, pa, w):
                ww = list(w) + ([self.vt_last_cP] if t2 == 0 else [])
                ca = self.opi("scalar", ww, "cA",
                              lambda e, t2=t2, pa=pa: e.activation(self.vtmp[:, t2 * 512:(t2 + 1) * 512],
                                                                   pa, AF.Copy))
                if t2 == 1:
                    self.emit_vt_transpose(m, self.vtmp, (H["cA"], ca))
                return (H["cA"], ca)
            self.emit_proj(self.wupvb, 8, ckv_rhs, v_evac, ckv_tok)
            def kc_evac(m, t2, pa, w):
                dst = self.kcs_tiles[m][:, t2 * 512:(t2 + 1) * 512]
                ca = self.opi("scalar", w, "cA", lambda e, dst=dst, pa=pa: e.activation(dst, pa, AF.Copy))
                return (H["cA"], ca)
            self.emit_proj(self.wupkb, 4, ckv_rhs, kc_evac, ckv_tok)
            kc_tok = self.tok("cA")
            for kvh in range(NKV):
                pair, par = kvh // 2, kvh % 2
                for t2 in range(2):
                    pa = self.psB[:, t2 * 512:(t2 + 1) * 512]
                    w = [kr_cv, kc_tok, self.psB_tok[t2], tab_tok]
                    def asm(e, pair=pair, par=par, t2=t2, pa=pa):
                        e.matmul(pa, self.t_perm[:, par * 128:(par + 1) * 128],
                                 self.kr0[:, pair * T + t2 * 512: pair * T + (t2 + 1) * 512],
                                 start=True, stop=False)
                        return e.matmul(pa, self.t_perm[:, 256 + par * 128:256 + (par + 1) * 128],
                                        self.kcs_tiles[pair][:, t2 * 512:(t2 + 1) * 512],
                                        start=False, stop=True)
                    pvv = self.opi("tensor", w, "cP", asm)
                    dst = self.kst[:, kvh * S + t2 * 512: kvh * S + (t2 + 1) * 512]
                    ca = self.opi("scalar", [(H["cP"], pvv)], "cA",
                                  lambda e, dst=dst, pa=pa: e.activation(dst, pa, AF.Copy))
                    self.psB_tok[t2] = (H["cA"], ca)

        xch_tok = self.emit_exchange("cc1" if isa else "cc2")

        self.qsp_tok = {0: None, 1: None}
        wq = self.wqa if isa else self.wqb
        def q_evac(m, t2, pa, w):
            par = m & 1
            qt = self.qtm[par]
            ww = list(w) + [self.scal_cv] + ([self.qsp_tok[par]] if t2 == 0 else [])
            ca = self.opi("scalar", ww, "cA",
                          lambda e, t2=t2, pa=pa, qt=qt:
                          e.activation(qt[:, t2 * 512:(t2 + 1) * 512], pa, AF.Copy,
                                       scale=self.scalB[:, 3:4]))
            if t2 == 1:
                cv = self.emit_rope(qt, self.t_cos, self.t_sin, [(H["cA"], ca)])
                da = self.opi("scalar", [(H["cV"], cv)], "dA",
                              lambda e, m=m, qt=qt: e.dma_start(out=self.qsp[m], in_=qt[:]), 16)
                self.qsp_tok[par] = (H["dA"], da)
            return (H["cA"], ca)
        self.emit_proj(wq, 32, lambda c, t2: self.st_chh(c, t2), q_evac, y_tok)
        qsp_done = self.tok("dA")

        kv_ready = [xch_tok, self.tok("cV"), self.tok("cA")]
        self.emit_attn_core(kv_ready, qsp_done)

        da = self.opi("sync", [self.tok("dA")], "dS",
                      lambda e: e.dma_start(
                          out=bass.AP(self.aarena, 0, [[DC * T, 128], [T, DC], [1, T]]),
                          in_=bass.AP(self.asp.tensor, 0, [[T, 128], [128 * T, DC], [1, T]])), 16)
        wo = self.woa if isa else self.wob
        def wo_evac(m, t2, pa, w):
            mu = self.muB[:, t2 * 512:(t2 + 1) * 512]
            sg = self.sgB[:, t2 * 512:(t2 + 1) * 512]
            ych = self.st_chh(m, t2)
            sf = self.scrF[:, 0:TH]
            self.op("vector", w, None, lambda e, ych=ych, sg=sg, sf=sf: e.tensor_mul(sf, ych, sg))
            self.op("vector", [], None, lambda e, pa=pa, sf=sf: e.tensor_add(sf, sf, pa))
            cv = self.opi("vector", [], "cV",
                          lambda e, ych=ych, sf=sf, mu=mu: e.tensor_add(ych, sf, mu))
            return (H["cV"], cv)
        self.emit_proj(wo, 32,
                       lambda c, t2: self.aarena[:, c * T + t2 * 512: c * T + (t2 + 1) * 512],
                       wo_evac, (H["dS"], da))
        return self.tok("cV")

    def emit_attn_core(self, kv_ready, qsp_done):
        H = self.H
        qb_tok = {0: None, 1: None}
        ab_tok = {0: None, 1: None}
        exp_tok = None
        trans_done = None
        first_core = list(kv_ready) + [qsp_done]
        for qg in range(2):
            own_pad = 512 * (qg + 1)
            wp = own_pad + 1024
            mk = self.opi("scalar", [self.tok("cP")], "dA",
                          lambda e, qg=qg: e.dma_start(
                              out=bass.AP(self.mstg, 0, [[4 * 1536, 128], [1536, 4], [1, 1536]]),
                              in_=bass.AP(self.masks.tensor, qg * 4 * 128 * 1536,
                                          [[1536, 128], [128 * 1536, 4], [1, 1536]])), 16)
            for h in range(NH):
                kv = h // 4
                par = h & 1
                qs_ap = self.qbuf[:, par * 512:(par + 1) * 512]
                qv = self.opi("scalar", [qb_tok[par], qsp_done], "dA",
                              lambda e, h=h, qg=qg, qs_ap=qs_ap:
                              e.dma_start(out=qs_ap, in_=self.qsp[h][:, qg * 512:(qg + 1) * 512]), 16)
                for qt in range(4):
                    j = qg * 4 + qt
                    w = [(H["dA"], qv), (H["dA"], mk)]
                    if qg == 0 and h == 0 and qt == 0:
                        w += first_core
                    if exp_tok is not None:
                        w.append(exp_tok)
                    def scr(e, qt=qt, kv=kv, wp=wp, qs_ap=qs_ap):
                        last = None
                        lh = qs_ap[:, qt * 128:(qt + 1) * 128]
                        for s0 in range(0, wp, 512):
                            last = e.matmul(self.psA[:, s0:s0 + 512], lh,
                                            self.kst[:, kv * S + s0: kv * S + s0 + 512],
                                            start=True, stop=True)
                        return last
                    pvv = self.opi("tensor", w, "cP", scr)
                    mw = min(1536, 2048 - 128 * j)
                    self.op("vector", [(H["cP"], pvv)], None,
                            lambda e, j=j, qt=qt, mw=mw: e.tensor_tensor(
                                self.psA[:, 128 * j:128 * j + mw], self.psA[:, 128 * j:128 * j + mw],
                                self.mstg[:, qt * 1536: qt * 1536 + mw], ALU.add))
                    cv = self.opi("vector", [], "cV",
                                  lambda e, wp=wp: e.reduce_max(self.nmx[:, 0:1], self.psA[:, 0:wp],
                                                                axis=AX, negate=True))
                    we = [(H["cV"], cv)]
                    if trans_done is not None:
                        we.append(trans_done)
                    ca = self.opi("scalar", we, "cA",
                                  lambda e, wp=wp: e.activation(self.psb[:, 0:wp], self.psA[:, 0:wp],
                                                                AF.Exp, bias=self.nmx[:, 0:1],
                                                                accum_out=self.nmx[:, 1:2]))
                    exp_tok = (H["cA"], ca)
                    cv = self.opi("vector", [exp_tok], "cV",
                                  lambda e: e.reciprocal(self.nmx[:, 2:3], self.nmx[:, 1:2]))
                    cv = self.opi("vector", [(H["cV"], cv)], "cV",
                                  lambda e, wp=wp: e.tensor_scalar_mul(self.psb[:, 0:wp],
                                                                       self.psb[:, 0:wp], self.nmx[:, 2:3]))
                    ptok = (H["cV"], cv)
                    nch = wp // 128
                    for b0 in range(0, nch, 4):
                        nb = min(4, nch - b0)
                        bsel = (b0 // 4) & 1
                        pt = self.psT[bsel]
                        def trs(e, b0=b0, nb=nb, pt=pt):
                            last = None
                            for k in range(nb):
                                last = e.transpose(pt[:, k * 128:(k + 1) * 128],
                                                   self.psb[:, (b0 + k) * 128:(b0 + k + 1) * 128],
                                                   self.t_id[:])
                            return last
                        pvv = self.opi("tensor", [ptok, self.psT_tok[bsel]], "cP", trs)
                        def ev(e, b0=b0, nb=nb, pt=pt, qt=qt):
                            dst = bass.AP(self.ptt, b0 * 512 + qt * 128,
                                          [[16 * 512, 128], [512, nb], [1, 128]])
                            return e.tensor_copy(dst, pt[:, 0:nb * 128].rearrange("p (c n) -> p c n", c=nb))
                        cvv = self.opi("vector", [(H["cP"], pvv)], "cV", ev)
                        self.psT_tok[bsel] = (H["cV"], cvv)
                    trans_done = self.tok("cP")
                pa = self.psB[:, par * 512:(par + 1) * 512]
                glist = list(range(wp // 128))
                w = [self.tok("cV"), self.psB_tok[par]]
                def pv(e, glist=glist, kv=kv, qg=qg, pa=pa):
                    last = None
                    for gi, g in enumerate(glist):
                        qs = 128 * max(0, (g - 8) - qg * 4) if g >= 8 else 0
                        last = e.matmul(pa[:, qs:512],
                                        self.vtt[:, g * 1024 + kv * 128: g * 1024 + (kv + 1) * 128],
                                        self.ptt[:, g * 512 + qs: (g + 1) * 512],
                                        start=(gi == 0), stop=(gi == len(glist) - 1))
                    return last
                pvv = self.opi("tensor", w, "cP", pv)
                ab = self.abuf[:, par * 512:(par + 1) * 512]
                cv = self.opi("vector", [(H["cP"], pvv), ab_tok[par]], "cV",
                              lambda e, ab=ab, pa=pa: e.tensor_copy(ab, pa))
                self.psB_tok[par] = (H["cV"], cv)
                da = self.opi("scalar", [(H["cV"], cv)], "dA",
                              lambda e, h=h, qg=qg, ab=ab:
                              e.dma_start(out=self.asp[h * 128:(h + 1) * 128, qg * 512:(qg + 1) * 512],
                                          in_=ab), 16)
                ab_tok[par] = (H["dA"], da)
                qb_tok[par] = self.tok("cP")

    def emit_ffn(self, y_tok, use_cache, write_cache, final):
        H = self.H
        self._in_ffn = True
        pv_toks = [None, None]
        out_dma_tok = None
        dsem = "dS" if use_cache else "dG"
        drec = "sync" if use_cache else "gpsimd"
        for th in range(2):
            for i in range(IC):
                par = i & 1
                gst = self.wsA if par == 0 else self.wsB
                ust = self.usA if par == 0 else self.usB
                gsrc = self.gcc[i] if use_cache else self.gw[i]
                usrc = self.ucc[i] if use_cache else self.uw[i]
                self.opi(drec, [self.ws_tok[par]], dsem,
                         lambda e, gst=gst, gsrc=gsrc: e.dma_start(out=gst[:], in_=gsrc), 16)
                dv = self.opi(drec, [self.us_tok[par]], dsem,
                              lambda e, ust=ust, usrc=usrc: e.dma_start(out=ust[:], in_=usrc), 16)
                if write_cache and th == 0:
                    self.opi("sync", [(H[dsem], dv)], "dS",
                             lambda e, gst=gst, i=i: e.dma_start(out=self.gcc[i], in_=gst[:]), 16)
                    self.opi("sync", [], "dS",
                             lambda e, ust=ust, i=i: e.dma_start(out=self.ucc[i], in_=ust[:]), 16)
                if MK_DUMP == "stage" and th == 0 and i in (0, 1, 2, 3):
                    self.opi("gpsimd", [(H[dsem], dv)], "dG",
                             lambda e, gst=gst, i=i: e.dma_start(
                                 out=self.out[1024 + 256 * i:1024 + 256 * i + 128, 0:1024],
                                 in_=gst[:, 0:1024]), 16)
                    self.opi("gpsimd", [], "dG",
                             lambda e, ust=ust, i=i: e.dma_start(
                                 out=self.out[1152 + 256 * i:1152 + 256 * i + 128, 0:1024],
                                 in_=ust[:, 0:1024]), 16)
                pg = self.pGH[par][:, 0:512]
                pu = self.pGH[par][:, 512:1024]
                w = [(H[dsem], dv), y_tok, pv_toks[par]]
                if i == 0:
                    w.append(self.ffn_prev_down)
                def mms(e, i=i, th=th, gst=gst, ust=ust, pg=pg, pu=pu):
                    for c in range(DC):
                        e.matmul(pg, gst[:, c * 128:(c + 1) * 128], self.st_chh(c, th),
                                 start=(c == 0), stop=(c == DC - 1))
                    last = None
                    for c in range(DC):
                        last = e.matmul(pu, ust[:, c * 128:(c + 1) * 128], self.st_chh(c, th),
                                        start=(c == 0), stop=(c == DC - 1))
                    return last
                pvv = self.opi("tensor", w, "cP", mms)
                self.ws_tok[par] = (H["cP"], pvv)
                self.us_tok[par] = (H["cP"], pvv)
                gsl = self.g_sb[:, par * TH:(par + 1) * TH]
                ca = self.opi("scalar", [(H["cP"], pvv), self.scal_cv], "cA",
                              lambda e, pg=pg, gsl=gsl: e.activation(gsl, pg, AF.Silu,
                                                                     scale=self.scalB[:, 0:1]))
                if MK_DUMP == "act" and th == 0 and i == 0:
                    cvd = self.opi("vector", [(H["cP"], pvv)], "cV",
                                   lambda e, pg=pg: e.tensor_copy(self.scrF[:, 0:TH], pg))
                    self.opi("gpsimd", [(H["cV"], cvd)], "dG",
                             lambda e: e.dma_start(out=self.out[128:256, 0:TH], in_=self.scrF[:, 0:TH]), 16)
                    self.opi("gpsimd", [(H["cA"], ca)], "dG",
                             lambda e, gsl=gsl: e.dma_start(out=self.out[256:384, 0:TH], in_=gsl), 16)
                if MK_DUMP == "act3" and th == 1 and i == 0:
                    cvd = self.opi("vector", [(H["cP"], pvv)], "cV",
                                   lambda e, pg=pg: e.tensor_copy(self.scrF[:, 0:TH], pg))
                    self.opi("gpsimd", [(H["cV"], cvd)], "dG",
                             lambda e: e.dma_start(out=self.out[2176:2304, 0:TH], in_=self.scrF[:, 0:TH]), 16)
                    self.opi("gpsimd", [(H["cA"], ca)], "dG",
                             lambda e, gsl=gsl: e.dma_start(out=self.out[2304:2432, 0:TH], in_=gsl), 16)
                    self.opi("gpsimd", [(H[dsem], dv)], "dG",
                             lambda e, gst=gst: e.dma_start(out=self.out[2432:2560, 0:1024], in_=gst[:, 0:1024]), 16)
                    self.opi("gpsimd", [(H["cV"], cvd)], "dG",
                             lambda e: e.dma_start(out=self.out[2560:2688, 0:TH], in_=self.st_chh(0, 1)), 16)
                if MK_DUMP == "act2" and th == 0 and i == 0:
                    cvd = self.opi("vector", [(H["cP"], pvv)], "cV",
                                   lambda e, pu=pu: e.tensor_copy(self.scrF[:, 0:TH], pu))
                    self.opi("gpsimd", [(H["cV"], cvd)], "dG",
                             lambda e: e.dma_start(out=self.out[384:512, 0:TH], in_=self.scrF[:, 0:TH]), 16)
                    cvd2 = self.opi("vector", [(H["cV"], cvd)], "cV",
                                    lambda e: e.tensor_copy(self.scrF[:, 600:608], self.scalB[:]))
                    self.opi("gpsimd", [(H["cV"], cvd2)], "dG",
                             lambda e: e.dma_start(out=self.out[512:640, 0:8], in_=self.scrF[:, 600:608]), 16)
                    self.opi("gpsimd", [], "dG",
                             lambda e: e.dma_start(out=self.out[640:768, 0:8], in_=self.scalB[:]), 16)
                hd = self.hst[:, i * TH:(i + 1) * TH]
                cv = self.opi("vector", [(H["cA"], ca)], "cV",
                              lambda e, hd=hd, pu=pu, gsl=gsl: e.scalar_tensor_tensor(
                                  hd, pu, self.scalB[:, 1:2], gsl, ALU.mult, ALU.mult))
                pv_toks[par] = (H["cV"], cv)
                if MK_DUMP == "h1" and th == 1 and i < 8:
                    self.opi("gpsimd", [(H["cV"], cv)], "dG",
                             lambda e, i=i, hd=hd: e.dma_start(
                                 out=self.out[i * 128:(i + 1) * 128, 512:1024], in_=hd), 16)
                if MK_DUMP == "h" and th == 0 and i < 8:
                    self.opi("gpsimd", [(H["cV"], cv)], "dG",
                             lambda e, i=i, hd=hd: e.dma_start(
                                 out=self.out[i * 128:(i + 1) * 128, 0:TH], in_=hd), 16)
            gu_done = self.tok("cV")
            ev_toks = [None, None]
            for g in range(8):
                pslot = g & 1
                px = self.pGH[pslot]
                mm_first = True
                for ib0 in range(0, IC, 8):
                    sz = min(8, IC - ib0)
                    par = (ib0 // 8) & 1
                    dstg = self.wsA if par == 0 else self.wsB
                    dten = self.dcc.tensor if use_cache else self.dwt.tensor
                    doff = (g * IC + ib0) * 128 * 512
                    dsrc = bass.AP(dten, doff, [[512, 128], [128 * 512, sz], [1, 512]])
                    ddst = bass.AP(dstg, 0, [[4096, 128], [512, sz], [1, 512]])
                    dv = self.opi(drec, [self.ws_tok[par], gu_done if (g == 0 and ib0 == 0) else None],
                                  dsem,
                                  lambda e, ddst=ddst, dsrc=dsrc:
                                  e.dma_start(out=ddst, in_=dsrc), 16)
                    if write_cache and th == 0:
                        csrc = bass.AP(self.dcc.tensor, doff, [[512, 128], [128 * 512, sz], [1, 512]])
                        self.opi("sync", [(H[dsem], dv)], "dS",
                                 lambda e, ddst=ddst, csrc=csrc:
                                 e.dma_start(out=csrc, in_=ddst), 16)
                    w = [(H[dsem], dv), gu_done]
                    if mm_first:
                        w.append(ev_toks[pslot])
                        if g < 2:
                            w.append(pv_toks[pslot])
                    def dmm(e, ib0=ib0, sz=sz, dstg=dstg, px=px, mm_first=mm_first, th=th):
                        last = None
                        for ii in range(sz):
                            i = ib0 + ii
                            for ncl in range(4):
                                last = e.matmul(px[:, ncl * 512:(ncl + 1) * 512],
                                                dstg[:, ii * 512 + ncl * 128: ii * 512 + (ncl + 1) * 128],
                                                self.hst[:, i * TH:(i + 1) * TH],
                                                start=(mm_first and ii == 0),
                                                stop=(ib0 + sz == IC and ii == sz - 1))
                        return last
                    pvv = self.opi("tensor", w, "cP", dmm)
                    self.ws_tok[par] = (H["cP"], pvv)
                    mm_first = False
                dn_tok = (H["cP"], self.cnt["cP"])
                if MK_DUMP == "down" and th == 0 and g == 0:
                    cvd = self.opi("vector", [dn_tok], "cV",
                                   lambda e, px=px: e.tensor_copy(self.scrF[:, 0:TH], px[:, 0:512]))
                    self.opi("gpsimd", [(H["cV"], cvd)], "dG",
                             lambda e: e.dma_start(out=self.out[2048:2176, 0:TH], in_=self.scrF[:, 0:TH]), 16)
                for ncl in range(4):
                    m = g * 4 + ncl
                    mu = self.muB[:, th * 512:(th + 1) * 512]
                    sg = self.sgB[:, th * 512:(th + 1) * 512]
                    ych = self.st_chh(m, th)
                    sf = self.scrF[:, 0:TH]
                    sfo = self.scrF[:, 512:1024]
                    pa = px[:, ncl * 512:(ncl + 1) * 512]
                    self.op("vector", [dn_tok] + ([out_dma_tok] if final else []), None,
                            lambda e, ych=ych, sg=sg, sf=sf: e.tensor_mul(sf, ych, sg))
                    self.op("vector", [self.scal_cv], None,
                            lambda e, pa=pa, sf=sf: e.scalar_tensor_tensor(
                                sf, pa, self.scalB[:, 2:3], sf, ALU.mult, ALU.add))
                    if final:
                        cv = self.opi("vector", [], "cV",
                                      lambda e, sf=sf, sfo=sfo, mu=mu: e.tensor_add(sfo, sf, mu))
                        da = self.opi("sync", [(H["cV"], cv)], "dS",
                                      lambda e, m=m, th=th, sfo=sfo: e.dma_start(
                                          out=self.out[m * 128:(m + 1) * 128, th * 512:(th + 1) * 512],
                                          in_=sfo), 16)
                        out_dma_tok = (H["dS"], da)
                    else:
                        cv = self.opi("vector", [], "cV",
                                      lambda e, ych=ych, sf=sf, mu=mu: e.tensor_add(ych, sf, mu))
                ev_toks[pslot] = (H["cV"], self.cnt["cV"])
            pv_toks = [self.tok("cV"), self.tok("cV")]
            self.ffn_prev_down = self.tok("cP")
            self.emit_barrier()
        self._in_ffn = False
        return self.tok("cV")

    def emit_barrier(self):
        H = self.H
        self.opi("tensor", [], "cP", lambda e: e.nop())
        self.opi("vector", [], "cV", lambda e: e.nop())
        self.opi("scalar", [], "cA", lambda e: e.nop())
        w = [self.tok("cP"), self.tok("cV"), self.tok("cA"),
             self.tok("dS"), self.tok("dG"), self.tok("dA")]
        for eng in ("tensor", "vector", "scalar", "gpsimd", "sync"):
            self.op(eng, w, None, lambda e: None)
        self.opi("tensor", [], "cP", lambda e: e.nop())
        self.opi("vector", [], "cV", lambda e: e.nop())
        self.opi("scalar", [], "cA", lambda e: e.nop())

    def emit_dump_stash(self):
        w = [self.tok("cV")]
        for c in range(DC):
            self.opi("gpsimd", w if c == 0 else [], "dG",
                     lambda e, c=c: e.dma_start(out=self.out[c * 128:(c + 1) * 128, :], in_=self.st_ch(c)), 16)
        self.op("sync", [self.tok("dG")], None, lambda e: None)

    def build(self):
        self.setup()
        t1 = self.emit_ln(self.rhoA, first=True)
        t1 = self.emit_attn("a", t1)
        if self.n_sub >= 2:
            t1 = self.emit_ln(self.rhoF)
            if MK_DUMP == "y2":
                self.emit_dump_stash()
                return self._finish()
            t1 = self.emit_ffn(t1, use_cache=False, write_cache=(self.n_sub >= 4), final=False)
            if MK_DUMP in ("h", "h1", "gate", "stage", "act", "act2", "act3", "down"):
                self.op("sync", [self.tok("dG")], None, lambda e: None)
                return self._finish()
        if self.n_sub >= 3:
            t1 = self.emit_ln(self.rhoA)
            t1 = self.emit_attn("b", t1)
        if self.n_sub >= 4:
            t1 = self.emit_ln(self.rhoF)
            t1 = self.emit_ffn(t1, use_cache=True, write_cache=False, final=True)
            self.op("sync", [self.tok("dS")], None, lambda e: None)
        else:
            self.emit_dump_stash()

        return self._finish()

    def _finish(self):
        with self.nc.Block() as block:
            for name, reg in (("tensor", block.tensor), ("vector", block.vector),
                              ("scalar", block.scalar), ("gpsimd", block.gpsimd),
                              ("sync", block.sync)):
                ops = self.ops[name]
                def runner(e, ops=ops):
                    for i0 in range(0, len(ops), CHUNK_OPS):
                        with e.Fori(0, 1):
                            for f in ops[i0:i0 + CHUNK_OPS]:
                                f(e)
                reg(runner)
        return self.nc


def _tile_lhsT(W):
    K, M = W.shape
    Kc, Mc = K // 128, M // 128
    return np.ascontiguousarray(W.reshape(Kc, 128, Mc, 128).transpose(2, 1, 0, 3))


def _bf(x):
    return np.asarray(x).astype(ml_dtypes.bfloat16)


def _host_prep(inputs):
    f32 = np.float32
    WqA = np.asarray(inputs["L0_attn_a_Wq"], f32)
    WkA = np.asarray(inputs["L0_attn_a_Wk"], f32)
    WvA = np.asarray(inputs["L0_attn_a_Wv"], f32)
    WoA = np.asarray(inputs["L0_attn_a_Wo"], f32)
    WqB = np.asarray(inputs["L0_attn_b_Wq"], f32)
    WkrB = np.asarray(inputs["L0_attn_b_Wkr"], f32)
    WdkB = np.asarray(inputs["L0_attn_b_Wdk"], f32)
    WupkB = np.asarray(inputs["L0_attn_b_Wupk"], f32)
    WupvB = np.asarray(inputs["L0_attn_b_Wupv"], f32)
    WoB = np.asarray(inputs["L0_attn_b_Wo"], f32)
    gwN = np.asarray(inputs["L0_ffn_gate_w"], np.int32)
    uwN = np.asarray(inputs["L0_ffn_up_w"], np.int32)
    dwN = np.asarray(inputs["L0_ffn_down_w"], np.int32)

    p128 = np.empty(128, np.int64)
    p128[0:64] = np.arange(0, 128, 2)
    p128[64:128] = np.arange(1, 128, 2)
    qperm = np.concatenate([h * 128 + p128 for h in range(NH)])
    kperm = np.concatenate([h * 128 + p128 for h in range(NKV)])
    krp = []
    for pr in range(4):
        h0, h1 = 2 * pr, 2 * pr + 1
        krp += list(range(64 * h0, 64 * h0 + 32)) + list(range(64 * h1, 64 * h1 + 32))
        krp += list(range(64 * h0 + 32, 64 * h0 + 64)) + list(range(64 * h1 + 32, 64 * h1 + 64))
    krp = np.array(krp)

    pre = {
        "wqa": _tile_lhsT(WqA[:, qperm]),
        "wka": _tile_lhsT(WkA[:, kperm]),
        "wva": _tile_lhsT(WvA),
        "woa": _tile_lhsT(WoA),
        "wqb": _tile_lhsT(np.ascontiguousarray(WqB.T)),
        "wkrb": _tile_lhsT(np.ascontiguousarray(WkrB.T)[:, krp]),
        "wdkb": _tile_lhsT(np.ascontiguousarray(WdkB.T)),
        "wupkb": _tile_lhsT(np.ascontiguousarray(WupkB.T)),
        "wupvb": _tile_lhsT(np.ascontiguousarray(WupvB.T)),
        "wob": _tile_lhsT(np.ascontiguousarray(WoB.T)),
        "gw": np.ascontiguousarray(gwN.reshape(86, 128, 32, 128).transpose(0, 3, 2, 1)),
        "uw": np.ascontiguousarray(uwN.reshape(86, 128, 32, 128).transpose(0, 3, 2, 1)),
        "dwt": np.ascontiguousarray(dwN.reshape(8, 512, 86, 128).transpose(0, 2, 3, 1)),
    }

    PRe = np.zeros((128, 128), f32)
    PRo = np.zeros((128, 128), f32)
    PNe = np.zeros((128, 128), f32)
    PNo = np.zeros((128, 128), f32)
    for i in range(32):
        PRe[i, i] = 1; PRe[64 + i, 64 + i] = 1
        PRo[32 + i, i] = 1; PRo[96 + i, 64 + i] = 1
        PNe[i, 32 + i] = 1; PNe[32 + i, 96 + i] = 1
        PNo[64 + i, 32 + i] = 1; PNo[96 + i, 96 + i] = 1
    pre["permm"] = _bf(np.concatenate([PRe, PRo, PNe, PNo], axis=1))
    pre["identi"] = _bf(np.eye(128, dtype=f32))
    ob = np.zeros((128, 2), f32)
    ob[:, 0] = 1.0
    pre["onesb"] = _bf(ob)
    pre["onesf"] = np.ones((1, 128), f32)
    pre["consts"] = np.array([[float(np.asarray(inputs["L0_ffn_gate_s"]).ravel()[0]),
                               float(np.asarray(inputs["L0_ffn_up_s"]).ravel()[0]),
                               float(np.asarray(inputs["L0_ffn_down_s"]).ravel()[0]),
                               INV_SQRT_D, EPS, 0.0, 0.0, 0.0]], f32)
    return pre


def _core_prep(inputs, core):
    f32 = np.float32
    b, hf = core // 2, core % 2
    x = np.asarray(inputs["L0_attn_a_x"], f32)[b]
    xin = np.ascontiguousarray(x[hf * T:(hf + 1) * T, :].T)

    pos = hf * T + np.arange(T, dtype=np.float64)
    inv = 10000.0 ** (-np.arange(64, dtype=np.float64) / 64.0)
    ridx = np.arange(128) % 64
    ang = pos[None, :] * inv[ridx][:, None]
    sgn = np.where(np.arange(128) < 64, 1.0, -1.0)[:, None]
    ropa = np.stack([np.cos(ang), sgn * np.sin(ang)]).astype(f32)
    ridx2 = np.arange(128) % 32
    ang2 = pos[None, :] * inv[ridx2][:, None]
    ropk = np.stack([np.cos(ang2), sgn * np.sin(ang2)]).astype(f32)

    m = np.zeros((8, 128, 1536), f32)
    for j in range(8):
        wp = 1024 + 512 * (j // 4 + 1)
        qg = hf * T + 128 * j + np.arange(128)
        kg = 128 * j + np.arange(1536)
        vis = kg[None, :] <= qg[:, None]
        m[j] = np.where(vis, 0.0, NEG)
        if wp - 128 * j < 1536:
            m[j][:, wp - 128 * j:] = 0.0
    return {"xin": xin, "ropa": _bf(ropa), "ropk": _bf(ropk), "masks": _bf(m)}


def kernel(**inputs):
    pre = _host_prep(inputs)
    in_maps = []
    for c in range(8):
        d = dict(pre)
        d.update(_core_prep(inputs, c))
        in_maps.append(d)
    mk = MK(N_SUB)
    nc = mk.build()
    res = run_bass_kernel_spmd(nc, in_maps, core_ids=list(range(8)))
    out = np.empty((4, S, D), np.float32)
    for c in range(8):
        b, hf = c // 2, c % 2
        out[b, hf * T:(hf + 1) * T, :] = np.asarray(res.results[c]["out"]).T
    return out


# version 03 (final)
# speedup vs baseline: 1.6071x; 1.3318x over previous
import os
import numpy as np
import ml_dtypes
import concourse.bass as bass
import concourse.mybir as mybir
from concourse.bass_utils import run_bass_kernel_spmd

F32 = mybir.dt.float32
BF16 = mybir.dt.bfloat16
I32 = mybir.dt.int32
AF = mybir.ActivationFunctionType
ALU = mybir.AluOpType
AX = mybir.AxisListType.X

D = 4096
DC = 32
T = 1024
TH = 512
S = 2048
IC = 86
NH = 32
NKV = 8
INV_SQRT_D = float(1.0 / np.sqrt(128.0))
EPS = 1e-5
NEG = -1e9

N_SUB = int(os.environ.get("MK_NSUB", "4"))
MK_DUMP = os.environ.get("MK_DUMP", "")
MK_LOCKSTEP = os.environ.get("MK_LOCKSTEP", "F")
CHUNK_OPS = 400


class MK:
    def __init__(self, n_sub=4):
        self.n_sub = n_sub
        nc = bass.Bass("TRN2", target_bir_lowering=False)
        self.nc = nc
        self.ops = {k: [] for k in ("tensor", "vector", "scalar", "gpsimd", "sync")}
        self._ctx = []

        din = lambda n, s, d: nc.dram_tensor(n, s, d, kind="ExternalInput").ap()
        self.xin = din("xin", [D, T], F32)
        self.wqa = din("wqa", [32, 128, 32, 128], F32)
        self.wka = din("wka", [8, 128, 32, 128], F32)
        self.wva = din("wva", [8, 128, 32, 128], F32)
        self.woa = din("woa", [32, 128, 32, 128], F32)
        self.wqb = din("wqb", [32, 128, 32, 128], F32)
        self.wkrb = din("wkrb", [4, 128, 32, 128], F32)
        self.wdkb = din("wdkb", [4, 128, 32, 128], F32)
        self.wupkb = din("wupkb", [4, 128, 4, 128], F32)
        self.wupvb = din("wupvb", [8, 128, 4, 128], F32)
        self.wob = din("wob", [32, 128, 32, 128], F32)
        self.gw = din("gw", [86, 128, 32, 128], I32)
        self.uw = din("uw", [86, 128, 32, 128], I32)
        self.dwt = din("dwt", [8, 86, 128, 512], I32)
        self.masks = din("masks", [8, 128, 1536], BF16)
        self.ropa = din("ropa", [2, 128, T], BF16)
        self.ropk = din("ropk", [2, 128, T], BF16)
        self.permm = din("permm", [128, 512], BF16)
        self.identi = din("identi", [128, 128], BF16)
        self.onesb = din("onesb", [128, 2], BF16)
        self.onesf = din("onesf", [1, 128], F32)
        self.consts = din("consts", [1, 8], F32)
        self.out = nc.dram_tensor("out", [D, T], F32, kind="ExternalOutput").ap()

        self.qsp = nc.dram_tensor("qsp", [32, 128, T], BF16).ap()
        self.asp = nc.dram_tensor("asp", [D, T], BF16).ap()
        self.ccin_t = nc.dram_tensor("ccin", [2048, T], BF16)
        self.ccout_t = nc.dram_tensor("ccout", [4096, T], BF16)
        self.gcc = nc.dram_tensor("gcc", [86, 128, 32, 128], BF16).ap()
        self.ucc = nc.dram_tensor("ucc", [86, 128, 32, 128], BF16).ap()
        self.dcc = nc.dram_tensor("dcc", [8, 86, 128, 512], BF16).ap()

        self.alloc_map = {}

        def sb(name, shape, dt):
            b0 = nc.sbuf_base
            cm = nc.sbuf_tensor(name, shape, dt)
            h = cm.__enter__()
            self._ctx.append(cm)
            self.alloc_map[name] = (b0, nc.sbuf_base)
            return h

        def psmm(name, shape, dt):
            cm = nc.psum_tensor(name, shape, dt)
            h = cm.__enter__()
            self._ctx.append(cm)
            return h

        self.stash = sb("stash", [128, DC * T], BF16)
        self.muB = sb("muB", [128, T], F32)
        self.sgB = sb("sgB", [128, T], F32)
        self.scrF = sb("scrF", [128, T], F32)
        self.xsq = sb("xsq", [128, 2 * T], BF16)
        self.row = sb("row", [1, T], F32)
        self.t_id = sb("t_id", [128, 128], BF16)
        self.t_ones_b = sb("t_ones_b", [128, 2], BF16)
        self.t_ones_f = sb("t_ones_f", [1, 128], F32)
        self.t_consts = sb("t_consts", [1, 8], F32)
        self.scalB = sb("scalB", [128, 8], F32)
        self.nmx = sb("nmx", [128, 4], F32)
        base_before_us = None
        self.wsA = sb("wsA", [128, 4096], BF16)
        self.wsB = sb("wsB", [128, 4096], BF16)
        base_before_us = nc.sbuf_base
        self.usA = sb("usA", [128, 4096], BF16)
        self.usB = sb("usB", [128, 4096], BF16)
        base_after_us = nc.sbuf_base

        nc.sbuf_base = base_before_us
        self.kst = sb("kst", [128, NKV * S], BF16)
        self.vtt = sb("vtt", [128, 16 * 1024], BF16)
        self.rhoA = sb("rhoA", [128, T], F32)
        sub_base = nc.sbuf_base
        self.t_cos = sb("t_cos", [128, T], BF16)
        self.t_sin = sb("t_sin", [128, T], BF16)
        self.t_cos2 = sb("t_cos2", [128, T], BF16)
        self.t_sin2 = sb("t_sin2", [128, T], BF16)
        self.t_perm = sb("t_perm", [128, 512], BF16)
        self.kr0 = sb("kr0", [128, 4 * T], BF16)
        self.ckv = sb("ckv", [128, 4 * T], BF16)
        self.vtmp = sb("vtmp", [128, T], BF16)
        self.qtm = [sb("qtmA", [128, T], BF16), sb("qtmB", [128, T], BF16)]
        self.tr1 = sb("tr1", [128, T], BF16)
        self.tr2 = sb("tr2", [128, T], BF16)
        self.kcs_tiles = [self.tr1, self.tr2, self.qtm[0], self.qtm[1]]
        qkv_end = nc.sbuf_base
        nc.sbuf_base = sub_base
        self.ptt = sb("ptt", [128, 16 * 512], BF16)
        self.psb = sb("psb", [128, 2048], BF16)
        self.mstg = sb("mstg", [128, 4 * 1536], BF16)
        self.qbuf = sb("qbuf", [128, 2 * 512], BF16)
        self.abuf = sb("abuf", [128, 1024], BF16)
        core_end = nc.sbuf_base
        attn_end = max(qkv_end, core_end)

        nc.sbuf_base = base_before_us
        self.aarena = sb("aarena", [128, DC * T], BF16)
        self.rhoW = sb("rhoW", [128, T], F32)
        wo_end = nc.sbuf_base

        nc.sbuf_base = base_after_us
        self.rhoF = sb("rhoF", [128, T], F32)
        nc.sbuf_base = base_after_us
        self.hst = sb("hst", [128, IC * TH], BF16)
        ffn_end = nc.sbuf_base
        self.g_sb = self.xsq

        nc.sbuf_base = max(attn_end, wo_end, ffn_end)
        assert nc.sbuf_base <= nc.sbuf_top, f"SBUF over budget: {nc.sbuf_base} > {nc.sbuf_top}"
        self.sbuf_used = nc.sbuf_base

        psum_base0 = nc.psum_base
        self.psA = psmm("psA", [128, 2048], F32)
        self.psB = psmm("psB", [128, 1024], F32)
        self.psT = [psmm("psT0", [128, 1024], BF16), psmm("psT1", [128, 1024], BF16)]
        nc.psum_base = psum_base0
        self.pGH = [psmm("pG", [128, 2048], F32), psmm("pH", [128, 2048], F32)]

        self.H = {}
        self.cnt = {}
        for nm in ("dS", "dG", "dA", "cP", "cV", "cA", "cc1", "cc2"):
            cm = nc.semaphore(nm)
            self.H[nm] = cm.__enter__()
            self._ctx.append(cm)
            self.cnt[nm] = 0

        self.ws_par = 0
        self.ws_tok = {0: None, 1: None}
        self.us_tok = {0: None, 1: None}
        self.psT_tok = [None, None]
        self.psB_tok = [None, None]
        self.vt_last_cP = None
        self.ffn_prev_down = None

    def op(self, eng, waits, inc, body):
        ls = MK_LOCKSTEP
        if ls == "FS" and getattr(self, "_in_ffn", False) and eng in ("gpsimd", "sync") :
            waits = list(waits) + [self.tok("cP")]
        add = False
        if ls == "1":
            add = True
        elif ls.startswith("F") and getattr(self, "_in_ffn", False):
            if ls == "F":
                add = True
            elif ls == "FP" and eng == "tensor":
                add = True
            elif ls == "FA" and eng == "scalar":
                add = True
            elif ls == "FV" and eng == "vector":
                add = True
            elif ls == "FG" and eng in ("gpsimd", "sync"):
                add = True
            elif ls == "FD" and eng in ("tensor", "vector", "scalar"):
                add = True
            elif ls == "FB" and eng in ("gpsimd", "sync"):
                add = True
            elif ls == "FAV" and eng in ("vector", "scalar"):
                add = True
        if add:
            waits = list(waits) + [self.tok(nm) for nm in ("cP", "cV", "cA")]
        waits = tuple(w for w in waits if w is not None)
        def f(e, waits=waits, inc=inc, body=body):
            for (sh, v) in waits:
                e.wait_ge(sh, v)
            ins = body(e)
            if inc is not None and ins is not None:
                ins.then_inc(inc[0], inc[1])
        self.ops[eng].append(f)

    def opi(self, eng, waits, nm, body, k=1):
        self.op(eng, waits, (self.H[nm], k), body)
        self.cnt[nm] += k
        return self.cnt[nm]

    def tok(self, nm):
        return (self.H[nm], self.cnt[nm])

    def st_ch(self, c):
        return self.stash[:, c * T:(c + 1) * T]

    def st_chh(self, c, th):
        return self.stash[:, c * T + th * TH: c * T + (th + 1) * TH]

    def setup(self):
        for dst, src in ((self.t_id[:], self.identi), (self.t_ones_b[:], self.onesb),
                         (self.t_ones_f[:], self.onesf), (self.t_consts[:], self.consts)):
            self.opi("sync", [], "dS", lambda e, d=dst, s=src: e.dma_start(out=d, in_=s), 16)
        v = self.tok("dS")
        self.opi("tensor", [v], "cP",
                 lambda e: e.matmul(self.psB[:, 0:8], self.t_ones_f[:1, :], self.t_consts[:1, :],
                                    start=True, stop=True))
        self.opi("vector", [self.tok("cP")], "cV",
                 lambda e: e.tensor_copy(self.scalB[:], self.psB[:, 0:8]))
        self.scal_cv = self.tok("cV")

    def emit_ln(self, rho, first=False):
        H = self.H
        if first:
            for c in range(DC):
                self.opi("gpsimd", [], "dG",
                         lambda e, c=c: e.dma_start(out=self.st_ch(c), in_=self.xin[c * 128:(c + 1) * 128, :]), 16)
            in_tok = self.tok("dG")
        else:
            in_tok = self.tok("cV")
        stats_cp = {}
        for c in range(DC):
            sl = (c & 1) * T
            w = [in_tok] if c == 0 else []
            if c >= 2:
                w.append((H["cP"], stats_cp[c - 2]))
            ca = self.opi("scalar", w, "cA",
                          lambda e, c=c, sl=sl: e.activation(self.xsq[:, sl:sl + T], self.st_ch(c), AF.Square))
            def stats(e, c=c, sl=sl):
                last = None
                for t2 in range(2):
                    last = e.matmul(self.psA[0:1, t2 * 512:(t2 + 1) * 512], self.t_ones_b[:, 0:1],
                                    self.st_chh(c, t2), start=(c == 0), stop=(c == DC - 1))
                for t2 in range(2):
                    last = e.matmul(self.psB[0:1, t2 * 512:(t2 + 1) * 512], self.t_ones_b[:, 0:1],
                                    self.xsq[:, sl + t2 * 512: sl + (t2 + 1) * 512],
                                    start=(c == 0), stop=(c == DC - 1))
                return last
            stats_cp[c] = self.opi("tensor", [(H["cA"], ca)], "cP", stats)
        self.opi("vector", [self.tok("cP")], "cV",
                 lambda e: e.tensor_copy(self.row[:], self.psA[0:1, 0:1024]))
        def bsum(e):
            e.matmul(self.psA[:, 1024:1536], self.t_ones_f[:1, :], self.row[:1, 0:512], start=True, stop=True)
            return e.matmul(self.psA[:, 1536:2048], self.t_ones_f[:1, :], self.row[:1, 512:1024],
                            start=True, stop=True)
        self.opi("tensor", [self.tok("cV")], "cP", bsum)
        self.opi("vector", [self.tok("cP")], "cV",
                 lambda e: e.tensor_scalar_mul(self.muB[:], self.psA[:, 1024:2048], 1.0 / D))
        self.opi("vector", [], "cV", lambda e: e.tensor_copy(self.row[:], self.psB[0:1, 0:1024]))
        def bsq(e):
            e.matmul(self.psB[:, 0:512], self.t_ones_f[:1, :], self.row[:1, 0:512], start=True, stop=True)
            return e.matmul(self.psB[:, 512:1024], self.t_ones_f[:1, :], self.row[:1, 512:1024],
                            start=True, stop=True)
        self.opi("tensor", [self.tok("cV")], "cP", bsq)
        self.opi("vector", [], "cV", lambda e: e.tensor_mul(self.scrF[:], self.muB[:], self.muB[:]))
        self.opi("vector", [self.tok("cP")], "cV",
                 lambda e, rho=rho: e.scalar_tensor_tensor(rho[:], self.psB[:], 1.0 / D, self.scrF[:],
                                                           ALU.mult, ALU.subtract))
        self.opi("scalar", [self.tok("cV"), self.scal_cv], "cA",
                 lambda e, rho=rho: e.activation(self.sgB[:], rho[:], AF.Sqrt, bias=self.scalB[:, 4:5]))
        self.opi("vector", [self.tok("cA")], "cV", lambda e, rho=rho: e.reciprocal(rho[:], self.sgB[:]))
        rv = self.tok("cV")
        for c in range(DC):
            self.op("vector", [rv] if c == 0 else [], None,
                    lambda e, c=c: e.tensor_sub(self.scrF[:], self.st_ch(c), self.muB[:]))
            self.opi("vector", [], "cV",
                     lambda e, c=c, rho=rho: e.tensor_mul(self.st_ch(c), self.scrF[:], rho[:]))
        return self.tok("cV")

    def emit_proj(self, wdram, n_chunks, rhs_fn, evac_fn, y_tok, cast=True):
        H = self.H
        kc = wdram.shape[2]
        for m in range(n_chunks):
            par = self.ws_par & 1
            self.ws_par += 1
            stg = self.wsA if par == 0 else self.wsB
            dsem = "dG" if cast else "dS"
            dv = self.opi("gpsimd" if cast else "sync", [self.ws_tok[par]], dsem,
                          lambda e, m=m, stg=stg, kc=kc, wdram=wdram:
                          e.dma_start(out=stg[:, 0:kc * 128], in_=wdram[m]), 16)
            for t2 in range(2):
                pa = self.psB[:, t2 * 512:(t2 + 1) * 512]
                w = [(H[dsem], dv), y_tok, self.psB_tok[t2]]
                def mms(e, m=m, t2=t2, stg=stg, pa=pa, kc=kc, rhs_fn=rhs_fn):
                    last = None
                    for c in range(kc):
                        last = e.matmul(pa, stg[:, c * 128:(c + 1) * 128], rhs_fn(c, t2),
                                        start=(c == 0), stop=(c == kc - 1))
                    return last
                pvv = self.opi("tensor", w, "cP", mms)
                if t2 == 1:
                    self.ws_tok[par] = (H["cP"], pvv)
                self.psB_tok[t2] = evac_fn(m, t2, pa, [(H["cP"], pvv)])

    def emit_rope(self, dst, cosT, sinT, waits):
        self.op("vector", waits, None,
                lambda e, dst=dst: e.tensor_mul(self.tr1[0:64, :], dst[64:128, :], sinT[64:128, :]))
        self.op("vector", [], None,
                lambda e, dst=dst: e.tensor_mul(self.tr1[64:128, :], dst[0:64, :], sinT[0:64, :]))
        self.op("vector", [], None, lambda e, dst=dst: e.tensor_mul(self.tr2[:], dst[:], cosT[:]))
        return self.opi("vector", [], "cV",
                        lambda e, dst=dst: e.tensor_add(dst[:], self.tr2[:], self.tr1[:]))

    def emit_vt_transpose(self, m, src, ca_tok):
        H = self.H
        for b4 in range(2):
            pt = self.psT[b4]
            def trs(e, b4=b4, pt=pt, src=src):
                last = None
                for k in range(4):
                    tc = b4 * 4 + k
                    last = e.transpose(pt[:, k * 128:(k + 1) * 128], src[:, tc * 128:(tc + 1) * 128],
                                       self.t_id[:])
                return last
            pvv = self.opi("tensor", [ca_tok, self.psT_tok[b4]], "cP", trs)
            self.vt_last_cP = (H["cP"], pvv)
            def ev(e, m=m, b4=b4, pt=pt):
                dst = bass.AP(self.vtt, (b4 * 4) * 1024 + m * 128,
                              [[16 * 1024, 128], [1024, 4], [1, 128]])
                return e.tensor_copy(dst, pt[:, 0:512].rearrange("p (c n) -> p c n", c=4))
            cvv = self.opi("vector", [(H["cP"], pvv)], "cV", ev)
            self.psT_tok[b4] = (H["cV"], cvv)

    def emit_exchange(self, ccsem):
        H = self.H
        ccin = self.ccin_t.ap()
        ccout = self.ccout_t.ap()
        w0 = [self.tok("cV"), self.tok("cA")]
        for hc in range(NKV):
            self.opi("gpsimd", w0 if hc == 0 else [], "dG",
                     lambda e, hc=hc, ccin=ccin: e.dma_start(
                         out=ccin[hc * 128:(hc + 1) * 128, :], in_=self.kst[:, hc * S:hc * S + T]), 16)
        for tc in range(8):
            self.opi("gpsimd", [], "dG",
                     lambda e, tc=tc, ccin=ccin: e.dma_start(
                         out=ccin[1024 + tc * 128:1024 + (tc + 1) * 128, :],
                         in_=self.vtt[:, tc * 1024:(tc + 1) * 1024]), 16)
        dv = self.tok("dG")
        def coll(e):
            return e.collective_compute(
                "AllGather", ALU.bypass,
                replica_groups=[[0, 1], [2, 3], [4, 5], [6, 7]],
                ins=[self.ccin_t.ap()], outs=[self.ccout_t.ap()])
        self.op("gpsimd", [dv], (H[ccsem], 1), coll)
        self.cnt[ccsem] += 1
        for r in range(2):
            for hc in range(NKV):
                self.opi("gpsimd", [(H[ccsem], 1)] if (r == 0 and hc == 0) else [], "dG",
                         lambda e, r=r, hc=hc, ccout=ccout: e.dma_start(
                             out=self.kst[:, hc * S + r * T: hc * S + (r + 1) * T],
                             in_=ccout[2048 * r + hc * 128: 2048 * r + (hc + 1) * 128, :]), 16)
            for tc in range(8):
                self.opi("gpsimd", [], "dG",
                         lambda e, r=r, tc=tc, ccout=ccout: e.dma_start(
                             out=self.vtt[:, (8 * r + tc) * 1024:(8 * r + tc + 1) * 1024],
                             in_=ccout[2048 * r + 1024 + tc * 128: 2048 * r + 1024 + (tc + 1) * 128, :]), 16)
        return self.tok("dG")

    def emit_attn(self, layer, y_tok):
        H = self.H
        isa = layer == "a"
        tabs = [(self.t_cos[:], self.ropa[0]), (self.t_sin[:], self.ropa[1])]
        if not isa:
            tabs += [(self.t_cos2[:], self.ropk[0]), (self.t_sin2[:], self.ropk[1]),
                     (self.t_perm[:], self.permm)]
        for dst, src in tabs:
            self.opi("sync", [], "dS", lambda e, d=dst, s=src: e.dma_start(out=d, in_=s), 16)
        tab_tok = self.tok("dS")

        if isa:
            def k_evac(m, t2, pa, w):
                dst = self.kst[:, m * S + t2 * 512: m * S + (t2 + 1) * 512]
                ca = self.opi("scalar", w, "cA", lambda e, dst=dst, pa=pa: e.activation(dst, pa, AF.Copy))
                return (H["cA"], ca)
            self.emit_proj(self.wka, 8, lambda c, t2: self.st_chh(c, t2), k_evac, y_tok)
            first = [tab_tok, self.tok("cA")]
            for m in range(NKV):
                self.emit_rope(self.kst[:, m * S:m * S + T], self.t_cos, self.t_sin, first)
                first = []
            def v_evac(m, t2, pa, w):
                ww = list(w) + ([self.vt_last_cP] if t2 == 0 else [])
                ca = self.opi("scalar", ww, "cA",
                              lambda e, t2=t2, pa=pa: e.activation(self.vtmp[:, t2 * 512:(t2 + 1) * 512],
                                                                   pa, AF.Copy))
                if t2 == 1:
                    self.emit_vt_transpose(m, self.vtmp, (H["cA"], ca))
                return (H["cA"], ca)
            self.emit_proj(self.wva, 8, lambda c, t2: self.st_chh(c, t2), v_evac, y_tok)
        else:
            def ckv_evac(m, t2, pa, w):
                dst = self.ckv[:, m * T + t2 * 512: m * T + (t2 + 1) * 512]
                ca = self.opi("scalar", w, "cA", lambda e, dst=dst, pa=pa: e.activation(dst, pa, AF.Copy))
                return (H["cA"], ca)
            self.emit_proj(self.wdkb, 4, lambda c, t2: self.st_chh(c, t2), ckv_evac, y_tok)
            ckv_tok = self.tok("cA")
            ckv_rhs = lambda c, t2: self.ckv[:, c * T + t2 * 512: c * T + (t2 + 1) * 512]
            def kr_evac(m, t2, pa, w):
                dst = self.kr0[:, m * T + t2 * 512: m * T + (t2 + 1) * 512]
                ca = self.opi("scalar", w, "cA", lambda e, dst=dst, pa=pa: e.activation(dst, pa, AF.Copy))
                return (H["cA"], ca)
            self.emit_proj(self.wkrb, 4, lambda c, t2: self.st_chh(c, t2), kr_evac, y_tok)
            first = [tab_tok, self.tok("cA")]
            for m in range(4):
                self.emit_rope(self.kr0[:, m * T:(m + 1) * T], self.t_cos2, self.t_sin2, first)
                first = []
            kr_cv = self.tok("cV")
            def v_evac(m, t2, pa, w):
                ww = list(w) + ([self.vt_last_cP] if t2 == 0 else [])
                ca = self.opi("scalar", ww, "cA",
                              lambda e, t2=t2, pa=pa: e.activation(self.vtmp[:, t2 * 512:(t2 + 1) * 512],
                                                                   pa, AF.Copy))
                if t2 == 1:
                    self.emit_vt_transpose(m, self.vtmp, (H["cA"], ca))
                return (H["cA"], ca)
            self.emit_proj(self.wupvb, 8, ckv_rhs, v_evac, ckv_tok)
            def kc_evac(m, t2, pa, w):
                dst = self.kcs_tiles[m][:, t2 * 512:(t2 + 1) * 512]
                ca = self.opi("scalar", w, "cA", lambda e, dst=dst, pa=pa: e.activation(dst, pa, AF.Copy))
                return (H["cA"], ca)
            self.emit_proj(self.wupkb, 4, ckv_rhs, kc_evac, ckv_tok)
            kc_tok = self.tok("cA")
            for kvh in range(NKV):
                pair, par = kvh // 2, kvh % 2
                for t2 in range(2):
                    pa = self.psB[:, t2 * 512:(t2 + 1) * 512]
                    w = [kr_cv, kc_tok, self.psB_tok[t2], tab_tok]
                    def asm(e, pair=pair, par=par, t2=t2, pa=pa):
                        e.matmul(pa, self.t_perm[:, par * 128:(par + 1) * 128],
                                 self.kr0[:, pair * T + t2 * 512: pair * T + (t2 + 1) * 512],
                                 start=True, stop=False)
                        return e.matmul(pa, self.t_perm[:, 256 + par * 128:256 + (par + 1) * 128],
                                        self.kcs_tiles[pair][:, t2 * 512:(t2 + 1) * 512],
                                        start=False, stop=True)
                    pvv = self.opi("tensor", w, "cP", asm)
                    dst = self.kst[:, kvh * S + t2 * 512: kvh * S + (t2 + 1) * 512]
                    ca = self.opi("scalar", [(H["cP"], pvv)], "cA",
                                  lambda e, dst=dst, pa=pa: e.activation(dst, pa, AF.Copy))
                    self.psB_tok[t2] = (H["cA"], ca)

        xch_tok = self.emit_exchange("cc1" if isa else "cc2")

        self.qsp_tok = {0: None, 1: None}
        wq = self.wqa if isa else self.wqb
        def q_evac(m, t2, pa, w):
            par = m & 1
            qt = self.qtm[par]
            ww = list(w) + [self.scal_cv] + ([self.qsp_tok[par]] if t2 == 0 else [])
            ca = self.opi("scalar", ww, "cA",
                          lambda e, t2=t2, pa=pa, qt=qt:
                          e.activation(qt[:, t2 * 512:(t2 + 1) * 512], pa, AF.Copy,
                                       scale=self.scalB[:, 3:4]))
            if t2 == 1:
                cv = self.emit_rope(qt, self.t_cos, self.t_sin, [(H["cA"], ca)])
                da = self.opi("scalar", [(H["cV"], cv)], "dA",
                              lambda e, m=m, qt=qt: e.dma_start(out=self.qsp[m], in_=qt[:]), 16)
                self.qsp_tok[par] = (H["dA"], da)
            return (H["cA"], ca)
        self.emit_proj(wq, 32, lambda c, t2: self.st_chh(c, t2), q_evac, y_tok)
        qsp_done = self.tok("dA")

        kv_ready = [xch_tok, self.tok("cV"), self.tok("cA")]
        self.emit_attn_core(kv_ready, qsp_done)

        da = self.opi("sync", [self.tok("dA")], "dS",
                      lambda e: e.dma_start(
                          out=bass.AP(self.aarena, 0, [[DC * T, 128], [T, DC], [1, T]]),
                          in_=bass.AP(self.asp.tensor, 0, [[T, 128], [128 * T, DC], [1, T]])), 16)
        wo = self.woa if isa else self.wob
        def wo_evac(m, t2, pa, w):
            mu = self.muB[:, t2 * 512:(t2 + 1) * 512]
            sg = self.sgB[:, t2 * 512:(t2 + 1) * 512]
            ych = self.st_chh(m, t2)
            sf = self.scrF[:, 0:TH]
            self.op("vector", w, None, lambda e, ych=ych, sg=sg, sf=sf: e.tensor_mul(sf, ych, sg))
            self.op("vector", [], None, lambda e, pa=pa, sf=sf: e.tensor_add(sf, sf, pa))
            cv = self.opi("vector", [], "cV",
                          lambda e, ych=ych, sf=sf, mu=mu: e.tensor_add(ych, sf, mu))
            return (H["cV"], cv)
        self.emit_proj(wo, 32,
                       lambda c, t2: self.aarena[:, c * T + t2 * 512: c * T + (t2 + 1) * 512],
                       wo_evac, (H["dS"], da))
        return self.tok("cV")

    def emit_attn_core(self, kv_ready, qsp_done):
        H = self.H
        qb_tok = {0: None, 1: None}
        ab_tok = {0: None, 1: None}
        exp_tok = None
        trans_done = None
        first_core = list(kv_ready) + [qsp_done]
        for qg in range(2):
            own_pad = 512 * (qg + 1)
            wp = own_pad + 1024
            mk = self.opi("scalar", [self.tok("cP")], "dA",
                          lambda e, qg=qg: e.dma_start(
                              out=bass.AP(self.mstg, 0, [[4 * 1536, 128], [1536, 4], [1, 1536]]),
                              in_=bass.AP(self.masks.tensor, qg * 4 * 128 * 1536,
                                          [[1536, 128], [128 * 1536, 4], [1, 1536]])), 16)
            for h in range(NH):
                kv = h // 4
                par = h & 1
                qs_ap = self.qbuf[:, par * 512:(par + 1) * 512]
                qv = self.opi("scalar", [qb_tok[par], qsp_done], "dA",
                              lambda e, h=h, qg=qg, qs_ap=qs_ap:
                              e.dma_start(out=qs_ap, in_=self.qsp[h][:, qg * 512:(qg + 1) * 512]), 16)
                for qt in range(4):
                    j = qg * 4 + qt
                    w = [(H["dA"], qv), (H["dA"], mk)]
                    if qg == 0 and h == 0 and qt == 0:
                        w += first_core
                    if exp_tok is not None:
                        w.append(exp_tok)
                    def scr(e, qt=qt, kv=kv, wp=wp, qs_ap=qs_ap):
                        last = None
                        lh = qs_ap[:, qt * 128:(qt + 1) * 128]
                        for s0 in range(0, wp, 512):
                            last = e.matmul(self.psA[:, s0:s0 + 512], lh,
                                            self.kst[:, kv * S + s0: kv * S + s0 + 512],
                                            start=True, stop=True)
                        return last
                    pvv = self.opi("tensor", w, "cP", scr)
                    mw = min(1536, 2048 - 128 * j)
                    self.op("vector", [(H["cP"], pvv)], None,
                            lambda e, j=j, qt=qt, mw=mw: e.tensor_tensor(
                                self.psA[:, 128 * j:128 * j + mw], self.psA[:, 128 * j:128 * j + mw],
                                self.mstg[:, qt * 1536: qt * 1536 + mw], ALU.add))
                    cv = self.opi("vector", [], "cV",
                                  lambda e, wp=wp: e.reduce_max(self.nmx[:, 0:1], self.psA[:, 0:wp],
                                                                axis=AX, negate=True))
                    we = [(H["cV"], cv)]
                    if trans_done is not None:
                        we.append(trans_done)
                    ca = self.opi("scalar", we, "cA",
                                  lambda e, wp=wp: e.activation(self.psb[:, 0:wp], self.psA[:, 0:wp],
                                                                AF.Exp, bias=self.nmx[:, 0:1],
                                                                accum_out=self.nmx[:, 1:2]))
                    exp_tok = (H["cA"], ca)
                    cv = self.opi("vector", [exp_tok], "cV",
                                  lambda e: e.reciprocal(self.nmx[:, 2:3], self.nmx[:, 1:2]))
                    cv = self.opi("vector", [(H["cV"], cv)], "cV",
                                  lambda e, wp=wp: e.tensor_scalar_mul(self.psb[:, 0:wp],
                                                                       self.psb[:, 0:wp], self.nmx[:, 2:3]))
                    ptok = (H["cV"], cv)
                    nch = wp // 128
                    for b0 in range(0, nch, 4):
                        nb = min(4, nch - b0)
                        bsel = (b0 // 4) & 1
                        pt = self.psT[bsel]
                        def trs(e, b0=b0, nb=nb, pt=pt):
                            last = None
                            for k in range(nb):
                                last = e.transpose(pt[:, k * 128:(k + 1) * 128],
                                                   self.psb[:, (b0 + k) * 128:(b0 + k + 1) * 128],
                                                   self.t_id[:])
                            return last
                        pvv = self.opi("tensor", [ptok, self.psT_tok[bsel]], "cP", trs)
                        def ev(e, b0=b0, nb=nb, pt=pt, qt=qt):
                            dst = bass.AP(self.ptt, b0 * 512 + qt * 128,
                                          [[16 * 512, 128], [512, nb], [1, 128]])
                            return e.tensor_copy(dst, pt[:, 0:nb * 128].rearrange("p (c n) -> p c n", c=nb))
                        cvv = self.opi("vector", [(H["cP"], pvv)], "cV", ev)
                        self.psT_tok[bsel] = (H["cV"], cvv)
                    trans_done = self.tok("cP")
                pa = self.psB[:, par * 512:(par + 1) * 512]
                glist = list(range(wp // 128))
                w = [self.tok("cV"), self.psB_tok[par]]
                def pv(e, glist=glist, kv=kv, qg=qg, pa=pa):
                    last = None
                    for gi, g in enumerate(glist):
                        qs = 128 * max(0, (g - 8) - qg * 4) if g >= 8 else 0
                        last = e.matmul(pa[:, qs:512],
                                        self.vtt[:, g * 1024 + kv * 128: g * 1024 + (kv + 1) * 128],
                                        self.ptt[:, g * 512 + qs: (g + 1) * 512],
                                        start=(gi == 0), stop=(gi == len(glist) - 1))
                    return last
                pvv = self.opi("tensor", w, "cP", pv)
                ab = self.abuf[:, par * 512:(par + 1) * 512]
                cv = self.opi("vector", [(H["cP"], pvv), ab_tok[par]], "cV",
                              lambda e, ab=ab, pa=pa: e.tensor_copy(ab, pa))
                self.psB_tok[par] = (H["cV"], cv)
                da = self.opi("scalar", [(H["cV"], cv)], "dA",
                              lambda e, h=h, qg=qg, ab=ab:
                              e.dma_start(out=self.asp[h * 128:(h + 1) * 128, qg * 512:(qg + 1) * 512],
                                          in_=ab), 16)
                ab_tok[par] = (H["dA"], da)
                qb_tok[par] = self.tok("cP")

    def emit_ffn(self, y_tok, use_cache, write_cache, final):
        H = self.H
        self._in_ffn = True
        pv_toks = [None, None]
        out_dma_tok = None
        dsem = "dS" if use_cache else "dG"
        drec = "sync" if use_cache else "gpsimd"
        for th in range(2):
            def rec_load(i):
                par = i & 1
                gst = self.wsA if par == 0 else self.wsB
                ust = self.usA if par == 0 else self.usB
                gsrc = self.gcc[i] if use_cache else self.gw[i]
                usrc = self.ucc[i] if use_cache else self.uw[i]
                self.opi(drec, [self.ws_tok[par]], dsem,
                         lambda e, gst=gst, gsrc=gsrc: e.dma_start(out=gst[:], in_=gsrc), 16)
                dv = self.opi(drec, [self.us_tok[par]], dsem,
                              lambda e, ust=ust, usrc=usrc: e.dma_start(out=ust[:], in_=usrc), 16)
                if write_cache and th == 0:
                    self.opi("sync", [(H[dsem], dv)], "dS",
                             lambda e, gst=gst, i=i: e.dma_start(out=self.gcc[i], in_=gst[:]), 16)
                    self.opi("sync", [], "dS",
                             lambda e, ust=ust, i=i: e.dma_start(out=self.ucc[i], in_=ust[:]), 16)
                return dv
            dv_pend = rec_load(0)
            for i in range(IC):
                par = i & 1
                gst = self.wsA if par == 0 else self.wsB
                ust = self.usA if par == 0 else self.usB
                dv = dv_pend
                if i + 1 < IC:
                    dv_pend = rec_load(i + 1)
                if MK_DUMP == "stage" and th == 0 and i in (0, 1, 2, 3):
                    self.opi("gpsimd", [(H[dsem], dv)], "dG",
                             lambda e, gst=gst, i=i: e.dma_start(
                                 out=self.out[1024 + 256 * i:1024 + 256 * i + 128, 0:1024],
                                 in_=gst[:, 0:1024]), 16)
                    self.opi("gpsimd", [], "dG",
                             lambda e, ust=ust, i=i: e.dma_start(
                                 out=self.out[1152 + 256 * i:1152 + 256 * i + 128, 0:1024],
                                 in_=ust[:, 0:1024]), 16)
                pg = self.pGH[par][:, 0:512]
                pu = self.pGH[par][:, 512:1024]
                w = [(H[dsem], dv), y_tok, pv_toks[par]]
                if i == 0:
                    w.append(self.ffn_prev_down)
                def mms(e, i=i, th=th, gst=gst, ust=ust, pg=pg, pu=pu):
                    for c in range(DC):
                        e.matmul(pg, gst[:, c * 128:(c + 1) * 128], self.st_chh(c, th),
                                 start=(c == 0), stop=(c == DC - 1))
                    last = None
                    for c in range(DC):
                        last = e.matmul(pu, ust[:, c * 128:(c + 1) * 128], self.st_chh(c, th),
                                        start=(c == 0), stop=(c == DC - 1))
                    return last
                pvv = self.opi("tensor", w, "cP", mms)
                self.ws_tok[par] = (H["cP"], pvv)
                self.us_tok[par] = (H["cP"], pvv)
                gsl = self.g_sb[:, par * TH:(par + 1) * TH]
                ca = self.opi("scalar", [(H["cP"], pvv), self.scal_cv], "cA",
                              lambda e, pg=pg, gsl=gsl: e.activation(gsl, pg, AF.Silu,
                                                                     scale=self.scalB[:, 0:1]))
                if MK_DUMP == "act" and th == 0 and i == 0:
                    cvd = self.opi("vector", [(H["cP"], pvv)], "cV",
                                   lambda e, pg=pg: e.tensor_copy(self.scrF[:, 0:TH], pg))
                    self.opi("gpsimd", [(H["cV"], cvd)], "dG",
                             lambda e: e.dma_start(out=self.out[128:256, 0:TH], in_=self.scrF[:, 0:TH]), 16)
                    self.opi("gpsimd", [(H["cA"], ca)], "dG",
                             lambda e, gsl=gsl: e.dma_start(out=self.out[256:384, 0:TH], in_=gsl), 16)
                if MK_DUMP == "act3" and th == 1 and i == 0:
                    cvd = self.opi("vector", [(H["cP"], pvv)], "cV",
                                   lambda e, pg=pg: e.tensor_copy(self.scrF[:, 0:TH], pg))
                    self.opi("gpsimd", [(H["cV"], cvd)], "dG",
                             lambda e: e.dma_start(out=self.out[2176:2304, 0:TH], in_=self.scrF[:, 0:TH]), 16)
                    self.opi("gpsimd", [(H["cA"], ca)], "dG",
                             lambda e, gsl=gsl: e.dma_start(out=self.out[2304:2432, 0:TH], in_=gsl), 16)
                    self.opi("gpsimd", [(H[dsem], dv)], "dG",
                             lambda e, gst=gst: e.dma_start(out=self.out[2432:2560, 0:1024], in_=gst[:, 0:1024]), 16)
                    self.opi("gpsimd", [(H["cV"], cvd)], "dG",
                             lambda e: e.dma_start(out=self.out[2560:2688, 0:TH], in_=self.st_chh(0, 1)), 16)
                if MK_DUMP == "act2" and th == 0 and i == 0:
                    cvd = self.opi("vector", [(H["cP"], pvv)], "cV",
                                   lambda e, pu=pu: e.tensor_copy(self.scrF[:, 0:TH], pu))
                    self.opi("gpsimd", [(H["cV"], cvd)], "dG",
                             lambda e: e.dma_start(out=self.out[384:512, 0:TH], in_=self.scrF[:, 0:TH]), 16)
                    cvd2 = self.opi("vector", [(H["cV"], cvd)], "cV",
                                    lambda e: e.tensor_copy(self.scrF[:, 600:608], self.scalB[:]))
                    self.opi("gpsimd", [(H["cV"], cvd2)], "dG",
                             lambda e: e.dma_start(out=self.out[512:640, 0:8], in_=self.scrF[:, 600:608]), 16)
                    self.opi("gpsimd", [], "dG",
                             lambda e: e.dma_start(out=self.out[640:768, 0:8], in_=self.scalB[:]), 16)
                hd = self.hst[:, i * TH:(i + 1) * TH]
                cv = self.opi("vector", [(H["cA"], ca)], "cV",
                              lambda e, hd=hd, pu=pu, gsl=gsl: e.scalar_tensor_tensor(
                                  hd, pu, self.scalB[:, 1:2], gsl, ALU.mult, ALU.mult))
                pv_toks[par] = (H["cV"], cv)
                if MK_DUMP == "h1" and th == 1 and i < 8:
                    self.opi("gpsimd", [(H["cV"], cv)], "dG",
                             lambda e, i=i, hd=hd: e.dma_start(
                                 out=self.out[i * 128:(i + 1) * 128, 512:1024], in_=hd), 16)
                if MK_DUMP == "h" and th == 0 and i < 8:
                    self.opi("gpsimd", [(H["cV"], cv)], "dG",
                             lambda e, i=i, hd=hd: e.dma_start(
                                 out=self.out[i * 128:(i + 1) * 128, 0:TH], in_=hd), 16)
            gu_done = self.tok("cV")
            ev_toks = [None, None]
            def rec_dload(g, ib0, first, bi):
                sz = min(8, IC - ib0)
                par = bi & 1
                dstg = self.wsA if par == 0 else self.wsB
                dten = self.dcc.tensor if use_cache else self.dwt.tensor
                doff = (g * IC + ib0) * 128 * 512
                dsrc = bass.AP(dten, doff, [[512, 128], [128 * 512, sz], [1, 512]])
                ddst = bass.AP(dstg, 0, [[4096, 128], [512, sz], [1, 512]])
                dv = self.opi(drec, [self.ws_tok[par], gu_done if first else None],
                              dsem,
                              lambda e, ddst=ddst, dsrc=dsrc:
                              e.dma_start(out=ddst, in_=dsrc), 16)
                if write_cache and th == 0:
                    csrc = bass.AP(self.dcc.tensor, doff, [[512, 128], [128 * 512, sz], [1, 512]])
                    self.opi("sync", [(H[dsem], dv)], "dS",
                             lambda e, ddst=ddst, csrc=csrc:
                             e.dma_start(out=csrc, in_=ddst), 16)
                return dv
            nblk = (IC + 7) // 8
            blist = [(g, ib0) for g in range(8) for ib0 in range(0, IC, 8)]
            dv_pend2 = rec_dload(0, 0, True, 0)
            for bi, (g, ib0) in enumerate(blist):
                pslot = g & 1
                px = self.pGH[pslot]
                mm_first = ib0 == 0
                if True:
                    sz = min(8, IC - ib0)
                    par = bi & 1
                    dstg = self.wsA if par == 0 else self.wsB
                    dv = dv_pend2
                    if bi + 1 < len(blist):
                        g2, ib2 = blist[bi + 1]
                        dv_pend2 = rec_dload(g2, ib2, False, bi + 1)
                    w = [(H[dsem], dv), gu_done]
                    if mm_first:
                        w.append(ev_toks[pslot])
                        if g < 2:
                            w.append(pv_toks[pslot])
                    def dmm(e, ib0=ib0, sz=sz, dstg=dstg, px=px, mm_first=mm_first, th=th):
                        last = None
                        for ii in range(sz):
                            i = ib0 + ii
                            for ncl in range(4):
                                last = e.matmul(px[:, ncl * 512:(ncl + 1) * 512],
                                                dstg[:, ii * 512 + ncl * 128: ii * 512 + (ncl + 1) * 128],
                                                self.hst[:, i * TH:(i + 1) * TH],
                                                start=(mm_first and ii == 0),
                                                stop=(ib0 + sz == IC and ii == sz - 1))
                        return last
                    pvv = self.opi("tensor", w, "cP", dmm)
                    self.ws_tok[par] = (H["cP"], pvv)
                if ib0 + 8 < IC:
                    continue
                dn_tok = (H["cP"], self.cnt["cP"])
                if MK_DUMP == "down" and th == 0 and g == 0:
                    cvd = self.opi("vector", [dn_tok], "cV",
                                   lambda e, px=px: e.tensor_copy(self.scrF[:, 0:TH], px[:, 0:512]))
                    self.opi("gpsimd", [(H["cV"], cvd)], "dG",
                             lambda e: e.dma_start(out=self.out[2048:2176, 0:TH], in_=self.scrF[:, 0:TH]), 16)
                for ncl in range(4):
                    m = g * 4 + ncl
                    mu = self.muB[:, th * 512:(th + 1) * 512]
                    sg = self.sgB[:, th * 512:(th + 1) * 512]
                    ych = self.st_chh(m, th)
                    sf = self.scrF[:, 0:TH]
                    sfo = self.scrF[:, 512:1024]
                    pa = px[:, ncl * 512:(ncl + 1) * 512]
                    self.op("vector", [dn_tok] + ([out_dma_tok] if final else []), None,
                            lambda e, ych=ych, sg=sg, sf=sf: e.tensor_mul(sf, ych, sg))
                    self.op("vector", [self.scal_cv], None,
                            lambda e, pa=pa, sf=sf: e.scalar_tensor_tensor(
                                sf, pa, self.scalB[:, 2:3], sf, ALU.mult, ALU.add))
                    if final:
                        cv = self.opi("vector", [], "cV",
                                      lambda e, sf=sf, sfo=sfo, mu=mu: e.tensor_add(sfo, sf, mu))
                        da = self.opi("sync", [(H["cV"], cv)], "dS",
                                      lambda e, m=m, th=th, sfo=sfo: e.dma_start(
                                          out=self.out[m * 128:(m + 1) * 128, th * 512:(th + 1) * 512],
                                          in_=sfo), 16)
                        out_dma_tok = (H["dS"], da)
                    else:
                        cv = self.opi("vector", [], "cV",
                                      lambda e, ych=ych, sf=sf, mu=mu: e.tensor_add(ych, sf, mu))
                ev_toks[pslot] = (H["cV"], self.cnt["cV"])
            pv_toks = [self.tok("cV"), self.tok("cV")]
            self.ffn_prev_down = self.tok("cP")
            self.emit_barrier()
        self._in_ffn = False
        return self.tok("cV")

    def emit_barrier(self):
        H = self.H
        self.opi("tensor", [], "cP", lambda e: e.nop())
        self.opi("vector", [], "cV", lambda e: e.nop())
        self.opi("scalar", [], "cA", lambda e: e.nop())
        w = [self.tok("cP"), self.tok("cV"), self.tok("cA"),
             self.tok("dS"), self.tok("dG"), self.tok("dA")]
        for eng in ("tensor", "vector", "scalar", "gpsimd", "sync"):
            self.op(eng, w, None, lambda e: None)
        self.opi("tensor", [], "cP", lambda e: e.nop())
        self.opi("vector", [], "cV", lambda e: e.nop())
        self.opi("scalar", [], "cA", lambda e: e.nop())

    def emit_dump_stash(self):
        w = [self.tok("cV")]
        for c in range(DC):
            self.opi("gpsimd", w if c == 0 else [], "dG",
                     lambda e, c=c: e.dma_start(out=self.out[c * 128:(c + 1) * 128, :], in_=self.st_ch(c)), 16)
        self.op("sync", [self.tok("dG")], None, lambda e: None)

    def build(self):
        self.setup()
        t1 = self.emit_ln(self.rhoA, first=True)
        t1 = self.emit_attn("a", t1)
        if self.n_sub >= 2:
            t1 = self.emit_ln(self.rhoF)
            if MK_DUMP == "y2":
                self.emit_dump_stash()
                return self._finish()
            t1 = self.emit_ffn(t1, use_cache=False, write_cache=(self.n_sub >= 4), final=False)
            if MK_DUMP in ("h", "h1", "gate", "stage", "act", "act2", "act3", "down"):
                self.op("sync", [self.tok("dG")], None, lambda e: None)
                return self._finish()
        if self.n_sub >= 3:
            t1 = self.emit_ln(self.rhoA)
            t1 = self.emit_attn("b", t1)
        if self.n_sub >= 4:
            t1 = self.emit_ln(self.rhoF)
            t1 = self.emit_ffn(t1, use_cache=True, write_cache=False, final=True)
            self.op("sync", [self.tok("dS")], None, lambda e: None)
        else:
            self.emit_dump_stash()

        return self._finish()

    def _finish(self):
        with self.nc.Block() as block:
            for name, reg in (("tensor", block.tensor), ("vector", block.vector),
                              ("scalar", block.scalar), ("gpsimd", block.gpsimd),
                              ("sync", block.sync)):
                ops = self.ops[name]
                def runner(e, ops=ops):
                    for i0 in range(0, len(ops), CHUNK_OPS):
                        with e.Fori(0, 1):
                            for f in ops[i0:i0 + CHUNK_OPS]:
                                f(e)
                reg(runner)
        return self.nc


def _tile_lhsT(W):
    K, M = W.shape
    Kc, Mc = K // 128, M // 128
    return np.ascontiguousarray(W.reshape(Kc, 128, Mc, 128).transpose(2, 1, 0, 3))


def _bf(x):
    return np.asarray(x).astype(ml_dtypes.bfloat16)


def _host_prep(inputs):
    f32 = np.float32
    WqA = np.asarray(inputs["L0_attn_a_Wq"], f32)
    WkA = np.asarray(inputs["L0_attn_a_Wk"], f32)
    WvA = np.asarray(inputs["L0_attn_a_Wv"], f32)
    WoA = np.asarray(inputs["L0_attn_a_Wo"], f32)
    WqB = np.asarray(inputs["L0_attn_b_Wq"], f32)
    WkrB = np.asarray(inputs["L0_attn_b_Wkr"], f32)
    WdkB = np.asarray(inputs["L0_attn_b_Wdk"], f32)
    WupkB = np.asarray(inputs["L0_attn_b_Wupk"], f32)
    WupvB = np.asarray(inputs["L0_attn_b_Wupv"], f32)
    WoB = np.asarray(inputs["L0_attn_b_Wo"], f32)
    gwN = np.asarray(inputs["L0_ffn_gate_w"], np.int32)
    uwN = np.asarray(inputs["L0_ffn_up_w"], np.int32)
    dwN = np.asarray(inputs["L0_ffn_down_w"], np.int32)

    p128 = np.empty(128, np.int64)
    p128[0:64] = np.arange(0, 128, 2)
    p128[64:128] = np.arange(1, 128, 2)
    qperm = np.concatenate([h * 128 + p128 for h in range(NH)])
    kperm = np.concatenate([h * 128 + p128 for h in range(NKV)])
    krp = []
    for pr in range(4):
        h0, h1 = 2 * pr, 2 * pr + 1
        krp += list(range(64 * h0, 64 * h0 + 32)) + list(range(64 * h1, 64 * h1 + 32))
        krp += list(range(64 * h0 + 32, 64 * h0 + 64)) + list(range(64 * h1 + 32, 64 * h1 + 64))
    krp = np.array(krp)

    pre = {
        "wqa": _tile_lhsT(WqA[:, qperm]),
        "wka": _tile_lhsT(WkA[:, kperm]),
        "wva": _tile_lhsT(WvA),
        "woa": _tile_lhsT(WoA),
        "wqb": _tile_lhsT(np.ascontiguousarray(WqB.T)),
        "wkrb": _tile_lhsT(np.ascontiguousarray(WkrB.T)[:, krp]),
        "wdkb": _tile_lhsT(np.ascontiguousarray(WdkB.T)),
        "wupkb": _tile_lhsT(np.ascontiguousarray(WupkB.T)),
        "wupvb": _tile_lhsT(np.ascontiguousarray(WupvB.T)),
        "wob": _tile_lhsT(np.ascontiguousarray(WoB.T)),
        "gw": np.ascontiguousarray(gwN.reshape(86, 128, 32, 128).transpose(0, 3, 2, 1)),
        "uw": np.ascontiguousarray(uwN.reshape(86, 128, 32, 128).transpose(0, 3, 2, 1)),
        "dwt": np.ascontiguousarray(dwN.reshape(8, 512, 86, 128).transpose(0, 2, 3, 1)),
    }

    PRe = np.zeros((128, 128), f32)
    PRo = np.zeros((128, 128), f32)
    PNe = np.zeros((128, 128), f32)
    PNo = np.zeros((128, 128), f32)
    for i in range(32):
        PRe[i, i] = 1; PRe[64 + i, 64 + i] = 1
        PRo[32 + i, i] = 1; PRo[96 + i, 64 + i] = 1
        PNe[i, 32 + i] = 1; PNe[32 + i, 96 + i] = 1
        PNo[64 + i, 32 + i] = 1; PNo[96 + i, 96 + i] = 1
    pre["permm"] = _bf(np.concatenate([PRe, PRo, PNe, PNo], axis=1))
    pre["identi"] = _bf(np.eye(128, dtype=f32))
    ob = np.zeros((128, 2), f32)
    ob[:, 0] = 1.0
    pre["onesb"] = _bf(ob)
    pre["onesf"] = np.ones((1, 128), f32)
    pre["consts"] = np.array([[float(np.asarray(inputs["L0_ffn_gate_s"]).ravel()[0]),
                               float(np.asarray(inputs["L0_ffn_up_s"]).ravel()[0]),
                               float(np.asarray(inputs["L0_ffn_down_s"]).ravel()[0]),
                               INV_SQRT_D, EPS, 0.0, 0.0, 0.0]], f32)
    return pre


def _core_prep(inputs, core):
    f32 = np.float32
    b, hf = core // 2, core % 2
    x = np.asarray(inputs["L0_attn_a_x"], f32)[b]
    xin = np.ascontiguousarray(x[hf * T:(hf + 1) * T, :].T)

    pos = hf * T + np.arange(T, dtype=np.float64)
    inv = 10000.0 ** (-np.arange(64, dtype=np.float64) / 64.0)
    ridx = np.arange(128) % 64
    ang = pos[None, :] * inv[ridx][:, None]
    sgn = np.where(np.arange(128) < 64, 1.0, -1.0)[:, None]
    ropa = np.stack([np.cos(ang), sgn * np.sin(ang)]).astype(f32)
    ridx2 = np.arange(128) % 32
    ang2 = pos[None, :] * inv[ridx2][:, None]
    ropk = np.stack([np.cos(ang2), sgn * np.sin(ang2)]).astype(f32)

    m = np.zeros((8, 128, 1536), f32)
    for j in range(8):
        wp = 1024 + 512 * (j // 4 + 1)
        qg = hf * T + 128 * j + np.arange(128)
        kg = 128 * j + np.arange(1536)
        vis = kg[None, :] <= qg[:, None]
        m[j] = np.where(vis, 0.0, NEG)
        if wp - 128 * j < 1536:
            m[j][:, wp - 128 * j:] = 0.0
    return {"xin": xin, "ropa": _bf(ropa), "ropk": _bf(ropk), "masks": _bf(m)}


def kernel(**inputs):
    pre = _host_prep(inputs)
    in_maps = []
    for c in range(8):
        d = dict(pre)
        d.update(_core_prep(inputs, c))
        in_maps.append(d)
    mk = MK(N_SUB)
    nc = mk.build()
    res = run_bass_kernel_spmd(nc, in_maps, core_ids=list(range(8)))
    out = np.empty((4, S, D), np.float32)
    for c in range(8):
        b, hf = c // 2, c % 2
        out[b, hf * T:(hf + 1) * T, :] = np.asarray(res.results[c]["out"]).T
    return out
